# version 21
# baseline (speedup 1.0000x reference)
import numpy as np
import ml_dtypes
import concourse.bass as bass
import concourse.mybir as mybir
from concourse.bass import IndirectOffsetOnAxis
from concourse.bass_utils import run_bass_kernel_spmd

F32 = mybir.dt.float32
BF16 = mybir.dt.bfloat16
U32 = mybir.dt.uint32
AF = mybir.ActivationFunctionType
OP = mybir.AluOpType
AX = mybir.AxisListType

NCORES = 8
B, L, D = 8, 2048, 1024
TA = B * L
ZDIM = 128
DTR = 64
NST = 16
E = 64
CAP = 512
TOPK = 8
ISH = 2048
IEXP = 512
MXF = 3200
EPS = 1e-5


ENGS = ("pe", "dve", "act", "pool", "sp")
STRICT_ALL = True
DEBUG_LINES = False


class Buf:
    __slots__ = ("name", "writer", "readers", "init_deps")

    def __init__(self, name, init_deps=()):
        self.name = name
        self.writer = None
        self.readers = []
        self.init_deps = list(init_deps)


class _Op:
    __slots__ = ("eng", "fn", "deps", "inc", "marked", "val", "strict", "selfwait",
                 "line")

    def __init__(self, eng, fn, inc, strict):
        import sys
        self.line = sys._getframe(3).f_lineno if DEBUG_LINES else 0
        self.eng = eng
        self.fn = fn
        self.deps = []
        self.inc = inc
        self.marked = inc >= 16
        self.val = None
        self.strict = strict
        self.selfwait = 0


class Plan:
    def __init__(self):
        self.ops = {e: [] for e in ENGS}
        self.order = []

    def op(self, eng, fn, reads=(), writes=(), dma=False, strict=False, extra_deps=()):
        o = _Op(eng, fn, 16 if dma else 1, strict or STRICT_ALL)
        for b in reads:
            if b.writer is not None:
                o.deps.append(b.writer)
            elif b.init_deps:
                o.deps.extend(b.init_deps)
        for b in writes:
            if b.writer is not None:
                o.deps.append(b.writer)
            elif b.init_deps:
                o.deps.extend(b.init_deps)
            o.deps.extend(b.readers)
        for d in extra_deps:
            if isinstance(d, Buf):
                if d.writer is not None:
                    o.deps.append(d.writer)
            elif d is not None:
                o.deps.append(d)
        for b in reads:
            b.readers.append(o)
        for b in writes:
            b.writer = o
            b.readers = []
        self.ops[eng].append(o)
        self.order.append(o)
        return o

    def finalize(self, block, sems, dsems):
        for o in self.order:
            for d in o.deps:
                if d.eng != o.eng or (o.strict and o.eng != "pe"):
                    d.marked = True
        semmap = {}
        for e2, h in sems.items():
            semmap[(e2,)] = h
        for e2, hl in dsems.items():
            for i, h in enumerate(hl):
                semmap[("d", e2, i)] = h
        cnt = {e: 0 for e in ENGS}
        dcnt = {e: 0 for e in ENGS}
        final = {}
        for e in ENGS:
            for o in self.ops[e]:
                if o.inc >= 16:
                    ns = len(dsems[e])
                    slot, rnd = dcnt[e] % ns, dcnt[e] // ns
                    o.val = (("d", e, slot), 16 * (rnd + 1))
                    o.selfwait = 16 * rnd
                    dcnt[e] += 1
                    final[o.val[0]] = o.val[1]
                elif o.marked:
                    cnt[e] += 1
                    o.val = ((e,), cnt[e])
                    final[(e,)] = cnt[e]

        def emit(e, name):
            water = {}
            for o in self.ops[name]:
                need = {}
                for d in o.deps:
                    if d.val is None:
                        continue
                    if (d.eng == name and d.inc < 16
                            and (not o.strict or name == "pe")):
                        continue
                    k, v = d.val
                    need[k] = max(need.get(k, 0), v)
                if o.inc >= 16 and o.selfwait > 0:
                    k = o.val[0]
                    need[k] = max(need.get(k, 0), o.selfwait)
                for k, v in sorted(need.items(), key=repr):
                    if v > water.get(k, 0):
                        e.wait_ge(semmap[k], v)
                        water[k] = v
                inst = o.fn(e)
                if o.val is not None:
                    if inst is None:
                        inst = e.engine_nop() if name in ("dve", "pool") else e.nop()
                    inst.then_inc(semmap[o.val[0]], o.inc)

        @block.tensor
        def _(e):
            emit(e, "pe")

        @block.vector
        def _(e):
            emit(e, "dve")

        @block.scalar
        def _(e):
            emit(e, "act")

        @block.gpsimd
        def _(e):
            emit(e, "pool")

        @block.sync
        def _(e):
            emit(e, "sp")
            for k, v in sorted(final.items(), key=repr):
                e.wait_ge(semmap[k], v)


class Arena:

    def __init__(self, plan, tensor, words):
        self.plan = plan
        self.t = tensor
        self.words = words
        self.off = 0
        self.live = []
        self.fence = []

    def reset(self):
        ops = []
        for b in self.live:
            if b.writer is not None:
                ops.append(b.writer)
            ops.extend(b.readers)
        self.fence = ops
        self.live = []
        self.off = 0

    def carve(self, name, words):
        aligned = (words + 15) & ~15
        assert self.off + aligned <= self.words, (
            f"arena overflow carving {name}: {self.off}+{aligned} > {self.words}")
        ap = self.t[:, self.off:self.off + words]
        self.off += aligned
        b = Buf(name, init_deps=self.fence)
        self.live.append(b)
        return ap, b

    def carve_f32(self, name, shape):
        fw = int(np.prod(shape[1:]))
        ap, b = self.carve(name, fw)
        if len(shape) == 3:
            ap = ap.rearrange("p (a c) -> p a c", a=shape[1])
        return ap, b

    def carve_bf16(self, name, shape):
        fw = int(np.prod(shape[1:]))
        ap, b = self.carve(name, (fw + 1) // 2)
        ap = ap.bitcast(BF16)
        if len(shape) == 3:
            ap = ap[:, :fw].rearrange("p (a c) -> p a c", a=shape[1])
        else:
            ap = ap[:, :fw]
        return ap, b

    def carve_u32(self, name, shape):
        fw = int(np.prod(shape[1:]))
        ap, b = self.carve(name, fw)
        ap = ap.bitcast(U32)
        if len(shape) == 3:
            ap = ap.rearrange("p (a c) -> p a c", a=shape[1])
        return ap, b



def _host_arrays(inputs):
    f32 = np.float32
    asc = np.ascontiguousarray
    a = {}
    a["x_full"] = asc(inputs["L0_attn_x"].reshape(TA, D).astype(f32, copy=False))
    a["norm_w"] = asc(np.asarray(inputs["L0_attn_norm_w"], f32).reshape(8, 128).T)
    a["v_wT"] = asc(np.asarray(inputs["L0_attn_v_w"], f32).T)
    a["v_b"] = asc(np.asarray(inputs["L0_attn_v_b"], f32).reshape(1, D))
    a["mx_wT"] = asc(np.asarray(inputs["L0_attn_mx_w"], f32).T)
    a["mx_b"] = asc(np.asarray(inputs["L0_attn_mx_b"], f32).reshape(MXF // 128, 128).T)
    a["h_wT"] = asc(np.asarray(inputs["L0_attn_h_w"], f32).T)
    a["h_b"] = asc(np.asarray(inputs["L0_attn_h_b"], f32).reshape(8, 128).T)
    g = np.asarray(inputs["L0_attn_gamma"], f32)
    bt = np.asarray(inputs["L0_attn_beta"], f32)
    a["g0_col"] = asc(g[0].reshape(128, 1)); a["g1_col"] = asc(g[1].reshape(128, 1))
    a["b0_col"] = asc(bt[0].reshape(128, 1)); a["b1_col"] = asc(bt[1].reshape(128, 1))
    al = np.asarray(inputs["L0_attn_alpha_rb"], f32).reshape(ZDIM)
    be = np.asarray(inputs["L0_attn_beta_rb"], f32).reshape(ZDIM)
    a["A1_col"] = asc(al.reshape(128, 1))
    a["A2_col"] = asc(np.concatenate([al[64:], al[:64]]).reshape(128, 1))
    a["B1_col"] = asc(be.reshape(128, 1))
    a["B2_col"] = asc(np.concatenate([be[64:], be[:64]]).reshape(128, 1))
    a["xproj_wT"] = asc(np.asarray(inputs["L0_attn_xproj_w"], f32).T)
    a["dt_wT"] = asc(np.asarray(inputs["L0_attn_dt_w"], f32).T)
    a["dt_b"] = asc(np.asarray(inputs["L0_attn_dt_b"], f32).reshape(8, 128).T)
    a["A_log"] = asc(np.asarray(inputs["L0_attn_A_log"], f32).reshape(8, 128, NST)
                     .transpose(1, 0, 2).reshape(128, 8 * NST))
    a["D_p"] = asc(np.asarray(inputs["L0_attn_D_p"], f32).reshape(8, 128).T)
    a["gate_w"] = asc(np.asarray(inputs["L0_ffn_gate_w"], f32))
    a["sg_w"] = asc(np.asarray(inputs["L0_ffn_shared_gate_w"], f32))
    a["sgu_w"] = asc(np.asarray(inputs["L0_ffn_shared_gate_up_w"], f32))
    a["sdown_w"] = asc(np.asarray(inputs["L0_ffn_shared_down_w"], f32))
    a["egu_w"] = asc(np.asarray(inputs["L0_ffn_expert_gate_up_w"], f32))
    a["edown_w"] = asc(np.asarray(inputs["L0_ffn_expert_down_w"], f32))
    a["identity"] = np.eye(128, dtype=f32)
    a["identity_bf"] = np.eye(128, dtype=ml_dtypes.bfloat16)
    a["ones_col_bf"] = np.ones((128, 1), ml_dtypes.bfloat16)
    a["tri_excl_bf"] = np.triu(np.ones((128, 128), f32), 1).astype(ml_dtypes.bfloat16)
    a["tri_excl_f32"] = np.triu(np.ones((128, 128), f32), 1)
    a["diagmask"] = np.triu(np.full((128, 128), -1e30, f32), 1)
    a["iota64_row"] = np.broadcast_to(np.arange(64, dtype=f32), (128, 64)).copy()
    a["iota128_col"] = np.arange(128, dtype=f32).reshape(128, 1)
    a["blkdiag"] = (np.arange(128)[:, None] // 8 == np.arange(16)[None, :]).astype(f32)
    half = ZDIM // 2
    freqs = np.exp(np.arange(half, dtype=np.float64) * (-np.log(10000.0) / half))
    ang = np.arange(L, dtype=np.float64)[:, None] * freqs[None, :]
    a["cos2"] = asc(np.concatenate([np.cos(ang).T, np.cos(ang).T], 0).astype(f32))
    a["sin2"] = asc(np.concatenate([-np.sin(ang).T, np.sin(ang).T], 0).astype(f32))
    return a



def build_program(debug_outs=()):
    nc = bass.Bass("TRN2", target_bir_lowering=False)
    from contextlib import ExitStack
    stk = ExitStack()
    plan = Plan()
    sems = {e: stk.enter_context(nc.semaphore(f"sem_{e}")) for e in ENGS}
    DSLOT = {"sp": 16, "pool": 8, "act": 4, "dve": 2, "pe": 2}
    dsems = {e: [stk.enter_context(nc.semaphore(f"dsem_{e}{i}"))
                 for i in range(DSLOT[e])] for e in ENGS}
    _inline = {}
    def dump_now(nm, ap2, b2, shp, dt2):
        if nm not in debug_outs:
            return
        od = nc.dram_tensor("dbg_" + nm, list(shp), dt2, kind="ExternalOutput").ap()
        plan.op("sp", lambda e: e.dma_start(out=od, in_=ap2), reads=[b2], dma=True)
        _inline[nm] = "dbg_" + nm
    _bregs = {}
    def bcreg(e, v):
        if v not in _bregs:
            _bregs[v] = e.to_reg(v)
        return _bregs[v]

    din = {}
    def dram_in(name, shape, dt=F32):
        din[name] = nc.dram_tensor(name, list(shape), dt, kind="ExternalInput").ap()
        return din[name]

    x_full = dram_in("x_full", (TA, D))
    x_own = dram_in("x_own", (L, D))
    dram_in("sel16", (128, 128))
    for nm, shp in [
        ("norm_w", (128, 8)), ("v_wT", (D, D)), ("v_b", (1, D)), ("mx_wT", (D, MXF)),
        ("mx_b", (128, MXF // 128)), ("h_wT", (D, D)), ("h_b", (128, 8)),
        ("g0_col", (128, 1)), ("g1_col", (128, 1)), ("b0_col", (128, 1)), ("b1_col", (128, 1)),
        ("A1_col", (128, 1)), ("A2_col", (128, 1)), ("B1_col", (128, 1)), ("B2_col", (128, 1)),
        ("xproj_wT", (D, 96)), ("dt_wT", (DTR, D)), ("dt_b", (128, 8)),
        ("A_log", (128, 8 * NST)), ("D_p", (128, 8)),
        ("gate_w", (D, E)), ("sg_w", (D, 1)), ("sgu_w", (D, 2 * ISH)), ("sdown_w", (ISH, D)),
        ("egu_w", (E, D, 2 * IEXP)), ("edown_w", (E, IEXP, D)),
        ("identity", (128, 128)), ("tri_excl_f32", (128, 128)),
        ("diagmask", (128, 128)), ("iota64_row", (128, 64)), ("iota128_col", (128, 1)),
        ("blkdiag", (128, 16)), ("cos2", (128, L)), ("sin2", (128, L)),
    ]:
        dram_in(nm, shp)
    dram_in("tri_excl_bf", (128, 128), BF16)
    dram_in("ones_col_bf", (128, 1), BF16)
    dram_in("identity_bf", (128, 128), BF16)
    dram_in("cpool", (128, 8))

    y_out = nc.dram_tensor("y", [L, D], F32, kind="ExternalOutput").ap()

    def dram_t(name, shape, dt=F32):
        return nc.dram_tensor(name, list(shape), dt).ap()

    xn_all = dram_t("xn_all", (TA, D), BF16)
    xn_own_d = dram_t("xn_own_d", (L, D), BF16)
    xnT_d = dram_t("xnT_d", (D, L))
    xn2T_d = dram_t("xn2T_d", (D, L))
    sharedT_d = dram_t("sharedT_d", (D, L))
    uT_d = dram_t("uT_d", (D, L))
    rT_d = dram_t("rT_d", (D, L))
    hxT_d = dram_t("hxT_d", (D, L))
    mxT_d = dram_t("mxT_d", (D, L), BF16)
    v_d = dram_t("v_d", (L, D), BF16)
    gtab_d = dram_t("gtab_d", (E * CAP, 1), U32)
    buf_d = dram_t("buf_d", (E * CAP, D), BF16)
    out_all = dram_t("out_all", (E * CAP, D), BF16)
    cidx_d = dram_t("cidx_d", (L * TOPK, 1), U32)
    wpr_d = dram_t("wpr_d", (L * TOPK, 1))
    ymoe_d = dram_t("ymoe_d", (L, D))
    toff_d = dram_t("toff_d", (128, E))
    toffown_d = dram_t("toffown_d", (16, E))
    tt_d = dram_t("tt_d", (1, 128 * E))
    negB_d = dram_t("negB_d", (NST, L))
    Cm_d = dram_t("Cm_d", (NST, L))
    grow_d = dram_t("grow_d", (1, L))
    sgu_bf = dram_t("sgu_bf", (D, 2 * ISH), BF16)
    sdown_bf = dram_t("sdown_bf", (ISH, D), BF16)
    vwT_bf = dram_t("vwT_bf", (D, D), BF16)
    mxwT_bf = dram_t("mxwT_bf", (D, MXF), BF16)
    hwT_bf = dram_t("hwT_bf", (D, D), BF16)

    dbuf = {}
    def db(t, r=None):
        key = (t.tensor.name, r)
        b = dbuf.get(key)
        if b is None:
            b = dbuf[key] = Buf(f"{key[0]}:{r}")
        return b
    def db_all(t):
        nm = t.tensor.name
        return [b for (n, _r), b in dbuf.items() if n == nm]

    pd0, pb_d0 = stk.enter_context(nc.psum_tensor("pd0", [128, 1024], F32)), Buf("pd0")
    pd1, pb_d1 = stk.enter_context(nc.psum_tensor("pd1", [128, 1024], F32)), Buf("pd1")
    pk = []
    for i in range(4):
        pk.append((stk.enter_context(nc.psum_tensor(f"pk{i}", [128, 512], F32)), Buf(f"pk{i}")))

    arA_t = stk.enter_context(nc.sbuf_tensor("arA", [128, 24576], F32))
    arB_t = stk.enter_context(nc.sbuf_tensor("arB", [128, 16384], F32))
    arA = Arena(plan, arA_t, 24576)
    arB = Arena(plan, arB_t, 16384)

    smalls = {}
    def small(name, shape, dt=F32):
        t = stk.enter_context(nc.sbuf_tensor(name, list(shape), dt))
        b = Buf(name)
        smalls[name] = (t, b)
        return t, b

    def ld(eng, dst_ap, dst_b, src_ap):
        plan.op(eng, lambda e: e.dma_start(out=dst_ap, in_=src_ap), writes=[dst_b], dma=True)

    c_ident, b_ident = small("c_ident", (128, 128))
    c_tri_bf, b_tri_bf = small("c_tri_bf", (128, 128), BF16)
    c_tri_f, b_tri_f = small("c_tri_f", (128, 128))
    c_diag, b_diag = small("c_diag", (128, 128))
    c_iota64, b_iota64 = small("c_iota64", (128, 64))
    c_iotac, b_iotac = small("c_iotac", (128, 1))
    c_blk, b_blk = small("c_blk", (128, 16))
    c_ones_bf, b_ones_bf = small("c_ones_bf", (128, 1), BF16)
    c_sel, b_sel = small("c_sel", (128, 128))
    c_identb, b_identb = small("c_identb", (128, 128), BF16)
    c_cpool, b_cpool = small("c_cpool", (128, 8))
    ld("sp", c_ident[:, :], b_ident, din["identity"])
    ld("sp", c_tri_bf[:, :], b_tri_bf, din["tri_excl_bf"])
    ld("sp", c_tri_f[:, :], b_tri_f, din["tri_excl_f32"])
    ld("sp", c_diag[:, :], b_diag, din["diagmask"])
    ld("sp", c_iota64[:, :], b_iota64, din["iota64_row"])
    ld("sp", c_iotac[:, :], b_iotac, din["iota128_col"])
    ld("sp", c_blk[:, :], b_blk, din["blkdiag"])
    ld("sp", c_ones_bf[:, :], b_ones_bf, din["ones_col_bf"])
    ld("sp", c_sel[:, :], b_sel, din["sel16"])
    ld("sp", c_identb[:, :], b_identb, din["identity_bf"])
    ld("sp", c_cpool[:, :], b_cpool, din["cpool"])
    _cvals = [1e-5, -1.0, -0.5, 1.0 / D, 1.0, 0.0, 1e-38, 511.0]
    for _i, _v in enumerate(_cvals):
        nc.const_aps.register(F32, _v, c_cpool[:, _i:_i + 1])
    for _e in ENGS:
        plan.op(_e, lambda e: None, reads=[b_cpool])

    w_gate, b_wgate = small("w_gate", (128, 8, E), BF16)
    plan.op("pool", lambda e: e.dma_start(
        out=w_gate[:, :, :], in_=din["gate_w"].rearrange("(hc p) e -> p hc e", p=128)),
        writes=[b_wgate], dma=True)
    w_xp, b_wxp = small("w_xp", (128, 8, 96), BF16)
    plan.op("pool", lambda e: e.dma_start(
        out=w_xp[:, :, :], in_=din["xproj_wT"].rearrange("(hc p) f -> p hc f", p=128)),
        writes=[b_wxp], dma=True)
    w_dt, b_wdt = small("w_dt", (64, 8, 128), BF16)
    plan.op("pool", lambda e: e.dma_start(
        out=w_dt[:, :, :], in_=din["dt_wT"].rearrange("r (c f) -> r c f", c=8)),
        writes=[b_wdt], dma=True)
    w_sg, b_wsg = small("w_sg", (128, 8, 1), BF16)
    plan.op("pool", lambda e: e.dma_start(
        out=w_sg[:, :, :], in_=din["sg_w"].rearrange("(hc p) o -> p hc o", p=128),
        allow_slow_non_contiguous=True),
        writes=[b_wsg], dma=True)
    w_vb, b_wvb = small("w_vb", (1, D), BF16)
    plan.op("pool", lambda e: e.dma_start(out=w_vb[:, :], in_=din["v_b"]),
            writes=[b_wvb], dma=True)
    s_dtin, b_dtin = small("s_dtin", (64, L), BF16)

    cols = {}
    for nm in ["g0_col", "g1_col", "b0_col", "b1_col", "A1_col", "A2_col", "B1_col",
               "B2_col", "dt_b", "h_b", "D_p", "A_log", "mx_b"]:
        shp = {"dt_b": (128, 8), "h_b": (128, 8), "D_p": (128, 8),
               "A_log": (128, 8 * NST), "mx_b": (128, MXF // 128)}.get(nm, (128, 1))
        t, b = small("s_" + nm, shp)
        cols[nm] = (t, b)
        ld("sp", t[tuple(slice(None) for _ in shp)], b, din[nm])
    t, b = small("s_norm_w", (128, 8))
    cols["norm_w"] = (t, b)
    ld("sp", t[:, :], b, din["norm_w"])

    s_expA, b_expA = small("s_expA", (128, 8 * NST))
    plan.op("act", lambda e: e.activation(s_expA[:, :], cols["A_log"][0][:, :], AF.Exp),
            reads=[cols["A_log"][1]], writes=[b_expA])
    s_ndtb, b_ndtb = small("s_ndtb", (128, 8))
    plan.op("dve", lambda e: e.tensor_scalar(s_ndtb[:, :], cols["dt_b"][0][:, :], -1.0, None, OP.mult),
            reads=[cols["dt_b"][1]], writes=[b_ndtb])
    s_qg, b_qg = small("s_qg", (128, 1))
    s_qb, b_qb = small("s_qb", (128, 1))
    sc128 = float(ZDIM) ** -0.5
    plan.op("dve", lambda e: e.tensor_scalar(s_qg[:, :], cols["g0_col"][0][:, :], sc128, None, OP.mult),
            reads=[cols["g0_col"][1]], writes=[b_qg])
    plan.op("dve", lambda e: e.tensor_scalar(s_qb[:, :], cols["b0_col"][0][:, :], sc128, None, OP.mult),
            reads=[cols["b0_col"][1]], writes=[b_qb])

    for srcn, dstt in [("sgu_w", sgu_bf), ("sdown_w", sdown_bf), ("v_wT", vwT_bf),
                       ("mx_wT", mxwT_bf), ("h_wT", hwT_bf)]:
        plan.op("pool", lambda e, s=srcn, d=dstt: e.dma_start(out=d, in_=din[s]),
                writes=[db(dstt)], dma=True)

    dbg = {}
    def dbg_dump(name, ap, shape, dt=F32, deps=()):
        if name not in debug_outs:
            return
        od = nc.dram_tensor("dbg_" + name, list(shape), dt, kind="ExternalOutput").ap()
        plan.op("sp", lambda e: e.dma_start(out=od, in_=ap), dma=True,
                extra_deps=list(deps))
        dbg[name] = od

    arA.reset(); arB.reset()
    invD = 1.0 / D
    s_x = []
    s_xnb = []
    for i in range(4):
        s_x.append(arA.carve_f32(f"s_x{i}", (128, D)))
    for i in range(2):
        s_xnb.append(arA.carve_bf16(f"s_xnb{i}", (128, D)))
    s_sq2 = [arA.carve_f32(f"s_sq{i}", (128, D)) for i in range(2)]
    s_lncol2 = [arA.carve_f32(f"s_lncol{i}", (128, 8)) for i in range(2)]
    s_xnf, b_xnf = arA.carve_f32("s_xnf", (128, D))
    s_x2s, b_x2s = arA.carve_f32("s_x2s", (128, D))
    s_tst, b_tst = arA.carve_f32("s_tst", (128, D))
    s_t2st, b_t2st = arA.carve_f32("s_t2st", (128, D))

    def emit_ln_tile(xsrc, xdst, k, own, xdst_b):
        sx, bx = s_x[k % 4]
        sxnb, bxnb = s_xnb[k % 2]
        s_sq, b_sq = s_sq2[k % 2]
        s_lncol, b_lncol = s_lncol2[k % 2]
        plan.op("sp", lambda e: e.dma_start(out=sx, in_=xsrc[k * 128:(k + 1) * 128, :]),
                writes=[bx], dma=True)
        plan.op("dve", lambda e: e.reduce_sum(s_lncol[:, 0:1], sx, AX.X),
                reads=[bx], writes=[b_lncol])
        plan.op("act", lambda e: e.activation(s_sq, sx, AF.Square, accum_out=s_lncol[:, 1:2]),
                reads=[bx], writes=[b_sq, b_lncol])
        plan.op("dve", lambda e: e.tensor_scalar(s_lncol[:, 2:3], s_lncol[:, 0:1], invD, None, OP.mult),
                reads=[b_lncol], writes=[b_lncol])
        plan.op("dve", lambda e: e.scalar_tensor_tensor(
            s_lncol[:, 3:4], s_lncol[:, 1:2], invD, s_lncol[:, 2:3], OP.mult, OP.bypass),
            reads=[b_lncol], writes=[b_lncol])
        plan.op("dve", lambda e: e.scalar_tensor_tensor(
            s_lncol[:, 3:4], s_lncol[:, 2:3], s_lncol[:, 2:3], s_lncol[:, 3:4],
            OP.mult, OP.subtract), reads=[b_lncol], writes=[b_lncol], strict=True)
        plan.op("act", lambda e: e.activation(s_lncol[:, 4:5], s_lncol[:, 3:4], AF.Ln,
                                              bias=EPS, scale=-1.0),
                reads=[b_lncol], writes=[b_lncol])
        plan.op("act", lambda e: e.activation(s_lncol[:, 4:5], s_lncol[:, 4:5], AF.Exp,
                                              scale=-0.5),
                reads=[b_lncol], writes=[b_lncol], strict=True)
        plan.op("dve", lambda e: e.scalar_tensor_tensor(
            s_lncol[:, 5:6], s_lncol[:, 2:3], -1.0, s_lncol[:, 4:5], OP.mult, OP.mult),
            reads=[b_lncol], writes=[b_lncol])
        plan.op("act", lambda e: e.activation(sxnb, sx, AF.Identity,
                                              bias=s_lncol[:, 5:6], scale=s_lncol[:, 4:5]),
                reads=[bx, b_lncol], writes=[bxnb])
        plan.op("pool", lambda e: e.dma_start(out=xdst[k * 128:(k + 1) * 128, :], in_=sxnb),
                reads=[bxnb], writes=[xdst_b], dma=True)
        if not own:
            return
        plan.op("act", lambda e: e.activation(s_xnf, sx, AF.Identity,
                                              bias=s_lncol[:, 5:6], scale=s_lncol[:, 4:5]),
                reads=[bx, b_lncol], writes=[b_xnf])
        plan.op("act", lambda e: e.activation(s_sq, s_xnf, AF.Square, accum_out=s_lncol[:, 6:7]),
                reads=[b_xnf], writes=[b_sq, b_lncol])
        plan.op("act", lambda e: e.activation(s_lncol[:, 6:7], s_lncol[:, 6:7], AF.Ln,
                                              bias=EPS, scale=invD),
                reads=[b_lncol], writes=[b_lncol], strict=True)
        plan.op("act", lambda e: e.activation(s_lncol[:, 6:7], s_lncol[:, 6:7], AF.Exp,
                                              scale=-0.5),
                reads=[b_lncol], writes=[b_lncol], strict=True)
        plan.op("act", lambda e: e.activation(s_x2s, s_xnf, AF.Identity, scale=s_lncol[:, 6:7]),
                reads=[b_xnf, b_lncol], writes=[b_x2s])
        for hc in range(8):
            pt, pbt = pk[hc % 2]
            plan.op("pe", lambda e, h=hc, p=pt: e.transpose(
                p[:, 0:128], s_xnf[:, h * 128:(h + 1) * 128], c_ident[:, :]),
                reads=[b_xnf, b_ident], writes=[pbt])
            plan.op("act", lambda e, h=hc, p=pt: e.copy(s_tst[:, h * 128:(h + 1) * 128], p[:, 0:128]),
                    reads=[pbt], writes=[b_tst])
            pt, pbt = pk[2 + hc % 2]
            plan.op("pe", lambda e, h=hc, p=pt: e.transpose(
                p[:, 0:128], s_x2s[:, h * 128:(h + 1) * 128], c_ident[:, :]),
                reads=[b_x2s, b_ident], writes=[pbt])
            plan.op("dve", lambda e, h=hc, p=pt: e.tensor_scalar(
                s_t2st[:, h * 128:(h + 1) * 128], p[:, 0:128],
                cols["norm_w"][0][:, h:h + 1], None, OP.mult),
                reads=[pbt, cols["norm_w"][1]], writes=[b_t2st])
        for hc in range(8):
            plan.op("sp", lambda e, h=hc: e.dma_start(
                out=xnT_d[h * 128:(h + 1) * 128, k * 128:(k + 1) * 128],
                in_=s_tst[:, h * 128:(h + 1) * 128]), reads=[b_tst], writes=[db(xnT_d, hc)], dma=True)
            plan.op("sp", lambda e, h=hc: e.dma_start(
                out=xn2T_d[h * 128:(h + 1) * 128, k * 128:(k + 1) * 128],
                in_=s_t2st[:, h * 128:(h + 1) * 128]), reads=[b_t2st], writes=[db(xn2T_d, hc)], dma=True)

    s_xnTt = []
    for i in range(2):
        ap, _b = arA.carve_bf16(f"s_xnTt{i}", (128, 8, 512))
        s_xnTt.append((ap, [Buf(f"s_xnTt{i}_h{h}", init_deps=_b.init_deps)
                            for h in range(8)]))
    s_lt, b_ltb = arA.carve_f32("s_lt", (128, 512))
    RT = []
    for _i in range(2):
        _d = {}
        _d["lg"], _d["b_lg"] = arA.carve_f32(f"s_lg{_i}", (128, 64))
        _d["m8"], _d["b_m8"] = arA.carve_f32(f"s_m8{_i}", (128, 8))
        _t, _b = small(f"s_i8t{_i}", (128, 8), U32)
        _d["i8"], _d["b_i8"] = _t[:, :], _b
        _d["idxf"], _d["b_idxf"] = arA.carve_f32(f"s_idxf{_i}", (128, 8))
        _d["cnta"], _d["b_cnta"] = arA.carve_f32(f"s_cnta{_i}", (128, 64))
        _d["cntb"], _d["b_cntb"] = arA.carve_f32(f"s_cntb{_i}", (128, 64))
        RT.append(_d)
    s_cntallt, b_cntall = small("s_cntallt", (128, 128, 64), BF16)
    s_cntall = s_cntallt[:, :, :]
    s_idxallt, b_idxall = small("s_idxallt", (128, 128, 8), U32)
    s_idxall = s_idxallt[:, :, :]

    def routing_tile_common(k_tile, cnt_store_ap, idx_store_ap, logits_psum_view, pb):
        R = RT[k_tile % 2]
        plan.op("dve", lambda e: e.tensor_copy(R["lg"], logits_psum_view),
                reads=[pb], writes=[R["b_lg"]])
        plan.op("dve", lambda e: e.max(R["m8"], R["lg"]),
                reads=[R["b_lg"]], writes=[R["b_m8"]])
        plan.op("dve", lambda e: e.max_index(R["i8"], R["m8"], R["lg"]),
                reads=[R["b_m8"], R["b_lg"]], writes=[R["b_i8"]], strict=True)
        plan.op("dve", lambda e: e.tensor_copy(R["idxf"], R["i8"]),
                reads=[R["b_i8"]], writes=[R["b_idxf"]])
        cur = (R["cnta"], R["b_cnta"])
        plan.op("dve", lambda e: e.tensor_scalar(
            R["cnta"], c_iota64[:, :], R["idxf"][:, 0:1], None, OP.is_equal),
            reads=[b_iota64, R["b_idxf"]], writes=[R["b_cnta"]])
        for j in range(1, 8):
            nxt = (R["cntb"], R["b_cntb"]) if j % 2 == 1 else (R["cnta"], R["b_cnta"])
            plan.op("dve", lambda e, jj=j, c=cur, n=nxt: e.scalar_tensor_tensor(
                n[0], c_iota64[:, :], R["idxf"][:, jj:jj + 1], c[0], OP.is_equal, OP.add),
                reads=[b_iota64, R["b_idxf"], cur[1]], writes=[nxt[1]])
            cur = nxt
        if k_tile == 0 and cnt_store_ap is not None:
            dump_now("idxf0", R["idxf"], R["b_idxf"], [128, 8], F32)
            dump_now("cntf0", cur[0], cur[1], [128, 64], F32)
        if cnt_store_ap is not None:
            plan.op("dve", lambda e: e.tensor_copy(cnt_store_ap, cur[0]),
                    reads=[cur[1]], writes=[b_cntall])
        if idx_store_ap is not None:
            plan.op("dve", lambda e: e.tensor_copy(idx_store_ap, R["i8"]),
                    reads=[R["b_i8"]], writes=[b_idxall])
        return cur, R

    def emit_logits_chunk(src_dram, tc, store_base_tile):
        sx, bxl = s_xnTt[tc % 2]
        for hc in range(8):
            plan.op("act", lambda e, h=hc: e.dma_start(
                out=sx[:, h, :],
                in_=src_dram[tc * 512:(tc + 1) * 512, h * 128:(h + 1) * 128],
                transpose=True), writes=[bxl[hc]], dma=True,
                extra_deps=db_all(src_dram))
        pl, pbl = pk[0]
        for hc in range(8):
            plan.op("pe", lambda e, h=hc: e.matmul(
                pl[0:64, :], w_gate[:, h, :], sx[:, h, :],
                start=(h == 0), stop=(h == 7)),
                reads=[b_wgate, bxl[hc]], writes=[pbl])
        plan.op("act", lambda e: e.copy(s_lt[0:64, :], pl[0:64, :]),
                reads=[pbl], writes=[b_ltb])
        for i in range(4):
            pt, pbt = pk[1]
            plan.op("pe", lambda e, ii=i: e.transpose(
                pt[:, 0:64], s_lt[0:64, ii * 128:(ii + 1) * 128], c_ident[0:64, 0:64]),
                reads=[b_ltb, b_ident], writes=[pbt])
            yield tc * 4 + i, pt[:, 0:64], pbt

    for k in range(TA // 128):
        emit_ln_tile(x_full, xn_all, k, False, db(xn_all, k % 8))
        if k % 4 == 3:
            tcc = k // 4
            for k_tile, pview, pb in emit_logits_chunk(xn_all, tcc, 0):
                routing_tile_common(k_tile, s_cntall[:, k_tile, :],
                                    s_idxall[:, k_tile, :], pview, pb)
    for k in range(L // 128):
        emit_ln_tile(x_own, xn_own_d, k, True, db(xn_own_d))

    dump_now("cnt_sb", s_cntall.rearrange("p a c -> p (a c)"), b_cntall, [128, 128 * 64], BF16)
    dump_now("idx_sb", s_idxall.rearrange("p a c -> p (a c)"), b_idxall, [128, 128 * 8], U32)

    s_ttrow, b_ttrow = arA.carve_f32("s_ttrow", (128, 512))
    cview = s_cntall.rearrange("p a c -> p (a c)")
    for i in range(16):
        pt, pbt = pk[2]
        plan.op("pe", lambda e, ii=i, p=pt: e.matmul(
            p[0:1, :], c_ones_bf[:, :], cview[:, ii * 512:(ii + 1) * 512],
            start=True, stop=True), reads=[b_ones_bf, b_cntall], writes=[pbt])
        plan.op("dve", lambda e, ii=i, p=pt: e.tensor_copy(s_ttrow[0:1, :], p[0:1, :]),
                reads=[pbt], writes=[b_ttrow])
        plan.op("sp", lambda e, ii=i: e.dma_start(
            out=tt_d[0:1, ii * 512:(ii + 1) * 512], in_=s_ttrow[0:1, :]),
            reads=[b_ttrow], writes=[db(tt_d)], dma=True)
    s_tt, b_tt = arA.carve_f32("s_tt", (128, 64))
    plan.op("sp", lambda e: e.dma_start(
        out=s_tt, in_=tt_d.rearrange("x (k c) -> (x k) c", k=128)),
        reads=[db(tt_d)], writes=[b_tt], dma=True)
    s_toff, b_toff = arA.carve_f32("s_toff", (128, 64))
    pt, pbt = pk[3]
    plan.op("pe", lambda e, p=pt: e.matmul(p[:, 0:64], c_tri_f[:, :], s_tt, start=True, stop=True),
            reads=[b_tri_f, b_tt], writes=[pbt])
    plan.op("dve", lambda e, p=pt: e.tensor_copy(s_toff, p[:, 0:64]), reads=[pbt], writes=[b_toff])
    plan.op("sp", lambda e: e.dma_start(out=toff_d, in_=s_toff),
            reads=[b_toff], writes=[db(toff_d)], dma=True)
    plan.op("pe", lambda e, p=pt: e.matmul(p[:, 64:128], c_sel[:, :], s_toff, start=True, stop=True),
            reads=[b_sel, b_toff], writes=[pbt])
    s_toffo, b_toffo = arA.carve_f32("s_toffo", (128, 64))
    plan.op("dve", lambda e, p=pt: e.tensor_copy(s_toffo[0:16, :], p[0:16, 64:128]),
            reads=[pbt], writes=[b_toffo])
    plan.op("sp", lambda e: e.dma_start(out=toffown_d, in_=s_toffo[0:16, :]),
            reads=[b_toffo], writes=[db(toffown_d)], dma=True)

    PB = []
    for _i in range(2):
        _d = {}
        for _nm, _shp in [("toffb", (128, 64)), ("pf", (128, 64)),
                          ("sc1", (128, 64)), ("pos", (128, 8)),
                          ("keep", (128, 8)), ("gx1", (128, 8)),
                          ("gcf", (128, 1)), ("idxf", (128, 8))]:
            _t, _b = small(f"s_pb_{_nm}{_i}", _shp)
            _d[_nm], _d["b_" + _nm] = _t[tuple(slice(None) for _ in _shp)], _b
        for _nm, _shp in [("gidx", (128, 8)), ("gcu", (128, 1))]:
            _t, _b = small(f"s_pb_{_nm}{_i}", _shp, U32)
            _d[_nm], _d["b_" + _nm] = _t[tuple(slice(None) for _ in _shp)], _b
        PB.append(_d)

    def emit_pos(k_tile, cnt_src_ap, b_cntsrc, b_idxf_src, idxf_ap, toff_dram,
                 toff_row, P):
        plan.op("sp", lambda e, st=P["toffb"]: e.dma_start(
            out=st, in_=toff_dram[toff_row:toff_row + 1, :].partition_broadcast(128)),
            writes=[P["b_toffb"]], dma=True, extra_deps=[db(toff_dram)])
        pp, pbp = pk[2]
        plan.op("pe", lambda e: e.matmul(pp[:, 0:64], c_tri_bf[:, :], cnt_src_ap,
                                         start=True, stop=True),
                reads=[b_tri_bf, b_cntsrc], writes=[pbp])
        plan.op("dve", lambda e, P2=P: e.tensor_tensor(
            P2["pf"], pp[:, 0:64], P2["toffb"], OP.add),
            reads=[pbp, P["b_toffb"]], writes=[P["b_pf"]])
        for j in range(8):
            plan.op("dve", lambda e, jj=j, P2=P, ix=idxf_ap: e.scalar_tensor_tensor(
                P2["sc1"], c_iota64[:, :], ix[:, jj:jj + 1], P2["pf"],
                OP.is_equal, OP.mult, accum_out=P2["pos"][:, jj:jj + 1]),
                reads=[b_iota64, b_idxf_src, P["b_pf"]],
                writes=[P["b_sc1"], P["b_pos"]])

    def gen_passB():
        for k in range(128):
            P = PB[k % 2]
            plan.op("dve", lambda e, kk=k, P2=P: e.tensor_copy(
                P2["idxf"], s_idxall[:, kk, :]),
                reads=[b_idxall], writes=[P["b_idxf"]])
            emit_pos(k, s_cntall[:, k, :], b_cntall, P["b_idxf"], P["idxf"],
                     toff_d, k, P)
            if k == 0:
                dump_now("pf0", P["pf"], P["b_pf"], [128, 64], F32)
                dump_now("pos0", P["pos"], P["b_pos"], [128, 8], F32)
            plan.op("dve", lambda e, P2=P: e.tensor_scalar(
                P2["keep"], P2["pos"], 512.0, None, OP.is_lt),
                reads=[P["b_pos"]], writes=[P["b_keep"]])
            plan.op("dve", lambda e, P2=P: e.scalar_tensor_tensor(
                P2["gx1"], P2["idxf"], 512.0, P2["pos"], OP.mult, OP.add),
                reads=[P["b_idxf"], P["b_pos"]], writes=[P["b_gx1"]])
            plan.op("dve", lambda e, P2=P: e.tensor_scalar(
                P2["gx1"], P2["gx1"], -70000.0, None, OP.add),
                reads=[P["b_gx1"]], writes=[P["b_gx1"]], strict=True)
            plan.op("dve", lambda e, P2=P: e.scalar_tensor_tensor(
                P2["gx1"], P2["keep"], 1.0, P2["gx1"], OP.bypass, OP.mult),
                reads=[P["b_keep"], P["b_gx1"]], writes=[P["b_gx1"]], strict=True)
            plan.op("dve", lambda e, P2=P: e.tensor_scalar(
                P2["gx1"], P2["gx1"], 70000.0, None, OP.add),
                reads=[P["b_gx1"]], writes=[P["b_gx1"]], strict=True)
            plan.op("dve", lambda e, P2=P: e.tensor_copy(P2["gidx"], P2["gx1"]),
                    reads=[P["b_gx1"]], writes=[P["b_gidx"]])
            plan.op("dve", lambda e, kk=k, P2=P: e.tensor_scalar(
                P2["gcf"], c_iotac[:, :], float(kk * 128), None, OP.add),
                reads=[b_iotac], writes=[P["b_gcf"]])
            plan.op("dve", lambda e, P2=P: e.tensor_copy(P2["gcu"], P2["gcf"]),
                    reads=[P["b_gcf"]], writes=[P["b_gcu"]])
            for j in range(8):
                plan.op("pool", lambda e, jj=j, P2=P: e.indirect_dma_start(
                    gtab_d, IndirectOffsetOnAxis(P2["gidx"][:, jj:jj + 1], 0),
                    P2["gcu"], None, bounds_check=bcreg(e, E * CAP - 1),
                    oob_is_err=False),
                    reads=[P["b_gidx"], P["b_gcu"]], writes=[db(gtab_d, j)],
                    dma=True)
            yield

    s_cob, b_cob = arA.carve_bf16("s_cob", (128, 64))
    s_wsum, b_wsum = arA.carve_f32("s_wsum", (128, 1))
    s_nm0, b_nm0 = arA.carve_f32("s_nm0", (128, 1))
    s_em8, b_em8 = arA.carve_f32("s_em8", (128, 8))
    s_w8, b_w8 = arA.carve_f32("s_w8", (128, 8))
    s_slot, b_slot = arA.carve_f32("s_slot", (128, 8))
    s_cif, b_cif = arA.carve_f32("s_cif", (128, 8))
    s_ciut, b_ciu = small("s_ciut", (128, 8), U32)
    s_ciu = s_ciut[:, :]

    for tc in range(4):
        for k_tile, pview, pb in emit_logits_chunk(xn_own_d, tc, 0):
            cur, R = routing_tile_common(k_tile, None, None, pview, pb)
            P = PB[k_tile % 2]
            plan.op("dve", lambda e, c=cur: e.tensor_copy(s_cob, c[0]),
                    reads=[cur[1]], writes=[b_cob])
            emit_pos(k_tile, s_cob, b_cob, R["b_idxf"], R["idxf"],
                     toffown_d, k_tile, P)
            plan.op("dve", lambda e, R2=R: e.tensor_scalar(
                s_nm0, R2["m8"][:, 0:1], -1.0, None, OP.mult),
                reads=[R["b_m8"]], writes=[b_nm0])
            plan.op("act", lambda e, R2=R: e.activation(
                s_em8, R2["m8"], AF.Exp, bias=s_nm0[:, 0:1]),
                reads=[R["b_m8"], b_nm0], writes=[b_em8])
            plan.op("dve", lambda e: e.reduce_sum(s_wsum, s_em8, AX.X),
                    reads=[b_em8], writes=[b_wsum])
            plan.op("dve", lambda e: e.reciprocal(s_wsum, s_wsum),
                    reads=[b_wsum], writes=[b_wsum], strict=True)
            plan.op("dve", lambda e: e.tensor_scalar(s_w8, s_em8, s_wsum[:, 0:1], None, OP.mult),
                    reads=[b_em8, b_wsum], writes=[b_w8])
            plan.op("dve", lambda e, P2=P: e.tensor_scalar(
                P2["keep"], P2["pos"], 512.0, None, OP.is_lt),
                reads=[P["b_pos"]], writes=[P["b_keep"]])
            plan.op("dve", lambda e, P2=P: e.tensor_tensor(
                s_w8, s_w8, P2["keep"], OP.mult),
                reads=[b_w8, P["b_keep"]], writes=[b_w8], strict=True)
            plan.op("dve", lambda e, P2=P: e.tensor_scalar(
                s_slot, P2["pos"], 511.0, None, OP.min),
                reads=[P["b_pos"]], writes=[b_slot])
            plan.op("dve", lambda e, R2=R: e.scalar_tensor_tensor(
                s_cif, R2["idxf"], 512.0, s_slot, OP.mult, OP.add),
                reads=[R["b_idxf"], b_slot], writes=[b_cif])
            plan.op("dve", lambda e: e.tensor_copy(s_ciu, s_cif), reads=[b_cif], writes=[b_ciu])
            plan.op("sp", lambda e, kk=k_tile: e.dma_start(
                out=cidx_d[kk * 1024:(kk + 1) * 1024, :].rearrange("(p j) x -> p (j x)", p=128),
                in_=s_ciu), reads=[b_ciu], writes=[db(cidx_d, k_tile % 4)], dma=True)
            plan.op("sp", lambda e, kk=k_tile: e.dma_start(
                out=wpr_d[kk * 1024:(kk + 1) * 1024, :].rearrange("(p j) x -> p (j x)", p=128),
                in_=s_w8), reads=[b_w8], writes=[db(wpr_d, k_tile % 4)], dma=True)

    arA.reset(); arB.reset()
    s_xnT, b_xnT = arA.carve_bf16("s_xnT", (128, 8, L))
    s_x2T, b_x2T = arA.carve_bf16("s_x2T", (128, 8, L))
    s_vw, b_vw = arA.carve_bf16("s_vw", (128, 8, D))
    s_nb, b_nb = arA.carve_f32("s_nb", (128, 512))
    s_cm, b_cm = arA.carve_f32("s_cm", (128, 512))
    s_vevac, b_vevac = arA.carve_bf16("s_vevac", (128, 1024))
    s_sdw, b_sdw = arB.carve_bf16("s_sdw", (128, 16, D))
    s_actT, b_actT = arB.carve_bf16("s_actT", (128, 16, 512))
    s_s1, b_s1 = arB.carve_f32("s_s1", (128, 512))
    s_wsgA = [arB.carve_bf16(f"s_wsg1{i}", (128, 8, 128)) for i in range(2)]
    s_wsgB = [arB.carve_bf16(f"s_wsg2{i}", (128, 8, 128)) for i in range(2)]
    s_gbc = [arB.carve_f32("s_gbc0", (128, 512))]
    s_grow, b_grow = arB.carve_f32("s_grow", (128, 512))
    s_shT, b_shT = arB.carve_f32("s_shT", (128, 512))

    plan.op("pool", lambda e: e.dma_start(
        out=s_xnT[:, :, :], in_=xnT_d.rearrange("(c p) t -> p c t", p=128)),
        reads=[], extra_deps=db_all(xnT_d), writes=[b_xnT], dma=True)
    plan.op("pool", lambda e: e.dma_start(
        out=s_x2T[:, :, :], in_=xn2T_d.rearrange("(c p) t -> p c t", p=128)),
        reads=[], extra_deps=db_all(xn2T_d), writes=[b_x2T], dma=True)
    plan.op("sp", lambda e: e.dma_start(
        out=s_vw[:, :, :], in_=vwT_bf.rearrange("(c p) d -> p c d", p=128)),
        reads=[db(vwT_bf)], writes=[b_vw], dma=True)
    plan.op("sp", lambda e: e.dma_start(
        out=s_sdw[:, :, :], in_=sdown_bf.rearrange("(c p) d -> p c d", p=128)),
        reads=[db(sdown_bf)], writes=[b_sdw], dma=True)

    def run_interleaved(*gens):
        gens = list(gens)
        while gens:
            for g2 in list(gens):
                try:
                    next(g2)
                except StopIteration:
                    gens.remove(g2)

    def gen_ph25a():
      for tc in range(4):
        pb0, pbb0 = pd0, pb_d0
        pb1, pbb1 = pd1, pb_d1
        for hc in range(8):
            plan.op("pe", lambda e, h=hc, t=tc: e.matmul(
                pb1[0:64, 0:512], w_xp[:, h, 0:64], s_x2T[:, h, t * 512:(t + 1) * 512],
                start=(h == 0), stop=(h == 7)), reads=[b_wxp, b_x2T], writes=[pbb1])
        for hc in range(8):
            plan.op("pe", lambda e, h=hc, t=tc: e.matmul(
                pb0[0:16, 0:512], w_xp[:, h, 64:80], s_x2T[:, h, t * 512:(t + 1) * 512],
                start=(h == 0), stop=(h == 7)), reads=[b_wxp, b_x2T], writes=[pbb0])
        for hc in range(8):
            plan.op("pe", lambda e, h=hc, t=tc: e.matmul(
                pb0[0:16, 512:1024], w_xp[:, h, 80:96], s_x2T[:, h, t * 512:(t + 1) * 512],
                start=(h == 0), stop=(h == 7)), reads=[b_wxp, b_x2T], writes=[pbb0])
        plan.op("act", lambda e, t=tc: e.copy(s_dtin[:, t * 512:(t + 1) * 512], pb1[0:64, 0:512]),
                reads=[pbb1], writes=[b_dtin])
        plan.op("act", lambda e: e.copy(s_nb[0:16, :], pb0[0:16, 0:512]),
                reads=[pbb0], writes=[b_nb])
        plan.op("act", lambda e: e.copy(s_cm[0:16, :], pb0[0:16, 512:1024]),
                reads=[pbb0], writes=[b_cm])
        plan.op("sp", lambda e, t=tc: e.dma_start(
            out=negB_d[:, t * 512:(t + 1) * 512], in_=s_nb[0:16, :]),
            reads=[b_nb], writes=[db(negB_d, tc % 2)], dma=True)
        plan.op("sp", lambda e, t=tc: e.dma_start(
            out=Cm_d[:, t * 512:(t + 1) * 512], in_=s_cm[0:16, :]),
            reads=[b_cm], writes=[db(Cm_d, tc % 2)], dma=True)
        yield

    dram_in("ones_row_bf", (1, 128), BF16)
    c_onesr, b_onesr = small("c_onesr", (1, 128), BF16)
    ld("sp", c_onesr[:, :], b_onesr, din["ones_row_bf"])

    def gen_ph25b():
      for tb in range(16):
        for dh in range(2):
            pv = pd0[:, dh * 512:(dh + 1) * 512]
            for hc in range(8):
                plan.op("pe", lambda e, h=hc, t=tb, d=dh: e.matmul(
                    pd0[:, d * 512:(d + 1) * 512], s_x2T[:, h, t * 128:(t + 1) * 128],
                    s_vw[:, h, d * 512:(d + 1) * 512],
                    start=(h == 0), stop=False), reads=[b_x2T, b_vw], writes=[pb_d0])
            plan.op("pe", lambda e, t=tb, d=dh: e.matmul(
                pd0[:, d * 512:(d + 1) * 512], c_onesr[:, :], w_vb[0:1, d * 512:(d + 1) * 512],
                start=False, stop=True), reads=[b_onesr, b_wvb], writes=[pb_d0])
            plan.op("act", lambda e, d=dh: e.activation(
                s_vevac[:, d * 512:(d + 1) * 512], pd0[:, d * 512:(d + 1) * 512], AF.Silu),
                reads=[pb_d0], writes=[b_vevac])
        plan.op("sp", lambda e, t=tb: e.dma_start(
            out=v_d[t * 128:(t + 1) * 128, :], in_=s_vevac),
            reads=[b_vevac], writes=[db(v_d, tb % 4)], dma=True)
        yield

    def gen_ph25c():
      for tc in range(4):
        pg, pbg = pk[3]
        for hc in range(8):
            plan.op("pe", lambda e, h=hc, t=tc: e.matmul(
                pg[0:1, 0:512], w_sg[:, h, :], s_xnT[:, h, t * 512:(t + 1) * 512],
                start=(h == 0), stop=(h == 7)), reads=[b_wsg, b_xnT], writes=[pbg])
        plan.op("act", lambda e: e.activation(s_grow[0:1, :], pg[0:1, 0:512], AF.Sigmoid),
                reads=[pbg], writes=[b_grow])
        plan.op("sp", lambda e, t=tc: e.dma_start(
            out=grow_d[0:1, t * 512:(t + 1) * 512], in_=s_grow[0:1, :]),
            reads=[b_grow], writes=[db(grow_d)], dma=True)
        yield

    def gen_ph25d():
      for tc in range(4):
        for fp in range(16):
            wg1, bwg1 = s_wsgA[fp % 2]
            wg2, bwg2 = s_wsgB[fp % 2]
            plan.op("sp", lambda e, f=fp, w=wg1: e.dma_start(
                out=w[:, :, :], in_=sgu_bf[:, f * 128:(f + 1) * 128]
                .rearrange("(c p) f -> p c f", p=128)),
                reads=[db(sgu_bf)], writes=[bwg1], dma=True)
            plan.op("sp", lambda e, f=fp, w=wg2: e.dma_start(
                out=w[:, :, :], in_=sgu_bf[:, ISH + f * 128:ISH + (f + 1) * 128]
                .rearrange("(c p) f -> p c f", p=128)),
                reads=[db(sgu_bf)], writes=[bwg2], dma=True)
            for hc in range(8):
                plan.op("pe", lambda e, h=hc, t=tc, w=wg1: e.matmul(
                    pk[0][0][:, :], w[:, h, :], s_xnT[:, h, t * 512:(t + 1) * 512],
                    start=(h == 0), stop=(h == 7)), reads=[bwg1, b_xnT], writes=[pk[0][1]])
            for hc in range(8):
                plan.op("pe", lambda e, h=hc, t=tc, w=wg2: e.matmul(
                    pk[1][0][:, :], w[:, h, :], s_xnT[:, h, t * 512:(t + 1) * 512],
                    start=(h == 0), stop=(h == 7)), reads=[bwg2, b_xnT], writes=[pk[1][1]])
            plan.op("act", lambda e: e.activation(s_s1, pk[0][0][:, :], AF.Silu),
                    reads=[pk[0][1]], writes=[b_s1])
            plan.op("dve", lambda e, f=fp: e.tensor_tensor(
                s_actT[:, f, :], s_s1, pk[1][0][:, :], OP.mult),
                reads=[b_s1, pk[1][1]], writes=[b_actT])
            yield
        plan.op("sp", lambda e, t=tc: e.dma_start(
            out=s_gbc[0][0],
            in_=grow_d[0:1, t * 512:(t + 1) * 512].partition_broadcast(128)),
            reads=[db(grow_d)], writes=[s_gbc[0][1]], dma=True)
        for dblk in range(8):
            pj, pbj = pk[2] if dblk % 2 == 0 else pk[3]
            for ic in range(16):
                plan.op("pe", lambda e, i=ic, d=dblk, p=pj: e.matmul(
                    p[:, :], s_sdw[:, i, d * 128:(d + 1) * 128], s_actT[:, i, :],
                    start=(i == 0), stop=(i == 15)), reads=[b_sdw, b_actT], writes=[pbj])
            plan.op("dve", lambda e, d=dblk, p=pj: e.tensor_tensor(
                s_shT, p[:, :], s_gbc[0][0], OP.mult),
                reads=[pbj, s_gbc[0][1]], writes=[b_shT])
            plan.op("sp", lambda e, d=dblk, t=tc: e.dma_start(
                out=sharedT_d[d * 128:(d + 1) * 128, t * 512:(t + 1) * 512], in_=s_shT),
                reads=[b_shT], writes=[db(sharedT_d, (tc * 8 + dblk) % 8)], dma=True)
            yield

    def gen_ph25():
        for g2 in (gen_ph25a(), gen_ph25b(), gen_ph25c(), gen_ph25d()):
            for _ in g2:
                yield

    run_interleaved(gen_passB(), gen_ph25())

    L_d = dram_t("L_d", (D, L))
    
    arB.reset()
    s_sg1, b_sg1 = arB.carve_f32("s_sg1", (128, 512))
    s_sg2, b_sg2 = arB.carve_f32("s_sg2", (128, 512))
    s_sg3, b_sg3 = arB.carve_f32("s_sg3", (128, 512))
    for tc in range(4):
        for g in range(8):
            pq, pbq = pk[g % 2]
            plan.op("pe", lambda e, gg=g, t=tc, p=pq: e.matmul(
                p[:, :], w_dt[:, gg, :], s_dtin[:, t * 512:(t + 1) * 512],
                start=True, stop=True), reads=[b_wdt, b_dtin], writes=[pbq])
            plan.op("act", lambda e, gg=g, p=pq: e.activation(
                s_sg1, p[:, :], AF.Sigmoid, scale=-1.0, bias=s_ndtb[:, gg:gg + 1]),
                reads=[pbq, b_ndtb], writes=[b_sg1])
            plan.op("dve", lambda e: e.tensor_scalar(s_sg2, s_sg1, 1e-38, None, OP.max),
                    reads=[b_sg1], writes=[b_sg2])
            plan.op("act", lambda e: e.activation(s_sg3, s_sg2, AF.Ln),
                    reads=[b_sg2], writes=[b_sg3])
            if tc == 0 and g == 0:
                dump_now("sg1d", s_sg1, b_sg1, [128, 512], F32)
                s_dtr, b_dtr = small("s_dtr", (128, 512))
                plan.op("act", lambda e, p=pq: e.copy(s_dtr[:, :], p[:, :]),
                        reads=[pbq], writes=[b_dtr])
                dump_now("dtrawd", s_dtr[:, :], b_dtr, [128, 512], F32)
                dump_now("wdtd", w_dt[:, :, :], b_wdt, [64, 8, 128], BF16)
                dump_now("ndtbd", s_ndtb[:, :], b_ndtb, [128, 8], F32)
            plan.op("sp", lambda e, gg=g, t=tc: e.dma_start(
                out=L_d[gg * 128:(gg + 1) * 128, t * 512:(t + 1) * 512], in_=s_sg3),
                reads=[b_sg3], writes=[db(L_d, (tc * 8 + g) % 8)], dma=True)

    arA.reset()
    s_uc = [arA.carve_f32("s_uc0", (128, 8, 512))]
    s_Lc = [arA.carve_f32("s_Lc0", (128, 8, 512))]
    s_du, b_du = arA.carve_f32("s_du", (128, 8, 512))
    s_ya, b_ya = arA.carve_f32("s_ya", (128, 8, 512))
    s_bcB = [arA.carve_f32(f"s_bcB{i}", (128, 512)) for i in range(2)]
    s_bcC = [arA.carve_f32(f"s_bcC{i}", (128, 512)) for i in range(2)]
    s_dA = [arA.carve_f32(f"s_dA{i}", (128, 512)) for i in range(4)]
    s_xB = [arA.carve_f32(f"s_xB{i}", (128, 512)) for i in range(2)]
    s_hh = [arA.carve_f32(f"s_hh{i}", (128, 512)) for i in range(2)]
    _thy, b_hy = small("s_hy_t", (128, 512))
    s_hy = _thy[:, :]
    s_cry = []
    for _i in range(2):
        _tc2, _bc2 = small(f"s_cry_t{_i}", (128, 128))
        s_cry.append((_tc2[:, :], _bc2))
    s_mxc, b_mxc = arA.carve_bf16("s_mxc", (128, 8, 512))

    def gen_scan():
      for c in range(4):
        su, bu = s_uc[0]
        sL, bL = s_Lc[0]
        plan.op("sp", lambda e, cc=c, d=su: e.dma_start(
            out=d[:, :, :], in_=xn2T_d.rearrange("(g p) t -> p g t", p=128)[:, :, cc * 512:(cc + 1) * 512]),
            reads=[], extra_deps=db_all(xn2T_d), writes=[bu], dma=True)
        plan.op("sp", lambda e, cc=c, d=sL: e.dma_start(
            out=d[:, :, :], in_=L_d.rearrange("(g p) t -> p g t", p=128)[:, :, cc * 512:(cc + 1) * 512]),
            reads=[], extra_deps=db_all(L_d), writes=[bL], dma=True)
        for g in range(8):
            plan.op("dve", lambda e, gg=g: e.scalar_tensor_tensor(
                s_du[:, gg, :], sL[:, gg, :], -1.0, su[:, gg, :], OP.mult, OP.mult),
                reads=[bL, bu], writes=[b_du])
            plan.op("dve", lambda e, gg=g: e.tensor_scalar(
                s_ya[:, gg, :], su[:, gg, :], cols["D_p"][0][:, gg:gg + 1], None, OP.mult),
                reads=[bu, cols["D_p"][1]], writes=[b_ya])
        for n in range(NST):
            sbB, bbB = s_bcB[n % 2]
            sbC, bbC = s_bcC[n % 2]
            plan.op("sp", lambda e, nn=n, cc=c, d=sbB: e.dma_start(
                out=d, in_=negB_d[nn:nn + 1, cc * 512:(cc + 1) * 512].partition_broadcast(128)),
                reads=[], extra_deps=db_all(negB_d), writes=[bbB], dma=True)
            plan.op("sp", lambda e, nn=n, cc=c, d=sbC: e.dma_start(
                out=d, in_=Cm_d[nn:nn + 1, cc * 512:(cc + 1) * 512].partition_broadcast(128)),
                reads=[], extra_deps=db_all(Cm_d), writes=[bbC], dma=True)
            for g in range(8):
                it = (n * 8 + g) % 2
                sdA, bdA = s_dA[(n * 8 + g) % 4]
                sxB, bxB = s_xB[it]
                shh, bhh = s_hh[it]
                col = g * NST + n
                plan.op("act", lambda e, gg=g, cl=col, d=sdA: e.activation(
                    d, sL[:, gg, :], AF.Exp, scale=s_expA[:, cl:cl + 1]),
                    reads=[bL, b_expA], writes=[bdA])
                plan.op("dve", lambda e, gg=g, d=sxB, sB=sbB: e.tensor_tensor(
                    d, s_du[:, gg, :], sB, OP.mult),
                    reads=[b_du, bbB], writes=[bxB])
                if c == 0:
                    plan.op("dve", lambda e, d=shh, a=sdA, x=sxB: e.tensor_tensor_scan(
                        d, a, x, 0.0, OP.mult, OP.add),
                        reads=[bdA, bxB], writes=[bhh])
                else:
                    cin, bcin = s_cry[(c + 1) % 2]
                    plan.op("dve", lambda e, d=shh, a=sdA, x=sxB, ci=cin, cl=col:
                            e.tensor_tensor_scan(d, a, x, ci[:, cl:cl + 1], OP.mult, OP.add),
                            reads=[bdA, bxB, bcin], writes=[bhh])
                cout, bcout = s_cry[c % 2]
                plan.op("dve", lambda e, d=shh, co=cout, cl=col: e.tensor_copy(
                    co[:, cl:cl + 1], d[:, 511:512]),
                    reads=[bhh], writes=[bcout])
                plan.op("dve", lambda e, d=shh, sC=sbC: e.tensor_tensor(
                    s_hy, d, sC, OP.mult), reads=[bhh, bbC], writes=[b_hy])
                plan.op("dve", lambda e, gg=g: e.tensor_tensor(
                    s_ya[:, gg, :], s_ya[:, gg, :], s_hy, OP.add),
                    reads=[b_hy, b_ya], writes=[b_ya], strict=True)
            yield
        for g in range(8):
            plan.op("act", lambda e, gg=g: e.activation(
                s_mxc[:, gg, :], s_ya[:, gg, :], AF.Silu),
                reads=[b_ya], writes=[b_mxc])
        plan.op("sp", lambda e, cc=c: e.dma_start(
            out=mxT_d.rearrange("(g p) t -> p g t", p=128)[:, :, cc * 512:(cc + 1) * 512],
            in_=s_mxc[:, :, :]), reads=[b_mxc], writes=[db(mxT_d, c)], dma=True)

    arB.reset()
    s_wg = [arB.carve_bf16(f"s_wg{i}", (128, 8, 1024)) for i in range(2)]
    s_wd, b_wd = arB.carve_bf16("s_wd", (128, 4, 1024))
    s_gtrawt, b_gtraw = small("s_gtrawt", (128, 128), U32)
    s_gtraw = s_gtrawt[:, :]
    s_gtTt, b_gtT = small("s_gtTt", (128, 256), U32)
    s_gtT = s_gtTt[:, :]
    s_braw = [arB.carve_bf16(f"s_braw{i}", (128, 1024)) for i in range(4)]
    s_bufT, b_bufT = arB.carve_bf16("s_bufT", (128, 8, 512))
    s_es1, b_es1 = arB.carve_f32("s_es1", (128, 512))
    s_eact, b_eact = arB.carve_bf16("s_eact", (128, 4, 512))
    s_eout, b_eout = arB.carve_bf16("s_eout", (128, 1024))

    for half in range(2):
        plan.op("sp", lambda e, hh=half: e.dma_start(
            out=s_gtraw,
            in_=gtab_d[hh * 16384:(hh + 1) * 16384, :].rearrange("(g i) x -> g (i x)", g=128)),
            reads=[], extra_deps=db_all(gtab_d), writes=[b_gtraw], dma=True)
        pt, pbt = pk[0]
        plan.op("pe", lambda e, p=pt: e.transpose(
            p[:, 0:128], s_gtraw.bitcast(F32), c_ident[:, :]),
            reads=[b_gtraw, b_ident], writes=[pbt])
        plan.op("dve", lambda e, hh=half, p=pt: e.tensor_copy(
            s_gtT.bitcast(F32)[:, hh * 128:(hh + 1) * 128], p[:, 0:128]),
            reads=[pbt], writes=[b_gtT], strict=True)

    def gen_experts():
      for ex in range(E):
        wg, bwg = s_wg[ex % 2]
        plan.op("pool", lambda e, x=ex, w=wg: e.dma_start(
            out=w[:, :, :], in_=din["egu_w"][x, :, :].rearrange("(hc p) f -> p hc f", p=128)),
            writes=[bwg], dma=True)
        for sbk in range(4):
            sbr, bbr = s_braw[sbk]
            gcol = ex * 4 + sbk
            plan.op("pool", lambda e, gc=gcol, d=sbr: e.indirect_dma_start(
                d, None, xn_all, IndirectOffsetOnAxis(s_gtT[:, gc:gc + 1], 0),
                bounds_check=bcreg(e, TA - 1), oob_is_err=False),
                reads=[b_gtT], extra_deps=db_all(xn_all), writes=[bbr], dma=True)
        for hc in range(8):
            ptb, pbtb = pk[2] if hc % 2 == 0 else pk[3]
            for sbk in range(4):
                sbr, bbr = s_braw[sbk]
                plan.op("pe", lambda e, h=hc, t=sbk, d=sbr, p=ptb: e.transpose(
                    p[:, t * 64:(t + 1) * 64].bitcast(BF16),
                    d[:, h * 128:(h + 1) * 128], c_identb[:, :]),
                    reads=[bbr, b_identb], writes=[pbtb])
            plan.op("act", lambda e, h=hc, p=ptb: e.copy(
                s_bufT[:, h, :], p[:, 0:256].bitcast(BF16)),
                reads=[pbtb], writes=[b_bufT], strict=True)
        plan.op("pool", lambda e, x=ex: e.dma_start(
            out=s_wd[:, :, :], in_=din["edown_w"][x, :, :].rearrange("(ic p) d -> p ic d", p=128)),
            writes=[b_wd], dma=True)
        for ic in range(4):
            pg_g, pbg_g = pk[0]
            pg_u, pbg_u = pk[1]
            for hc in range(8):
                plan.op("pe", lambda e, h=hc, i=ic, w=wg: e.matmul(
                    pg_g[:, :], w[:, h, i * 128:(i + 1) * 128], s_bufT[:, h, :],
                    start=(h == 0), stop=(h == 7)), reads=[bwg, b_bufT], writes=[pbg_g])
            for hc in range(8):
                plan.op("pe", lambda e, h=hc, i=ic, w=wg: e.matmul(
                    pg_u[:, :], w[:, h, 512 + i * 128:512 + (i + 1) * 128], s_bufT[:, h, :],
                    start=(h == 0), stop=(h == 7)), reads=[bwg, b_bufT], writes=[pbg_u])
            plan.op("act", lambda e: e.activation(s_es1, pg_g[:, :], AF.Silu),
                    reads=[pbg_g], writes=[b_es1])
            plan.op("dve", lambda e, i=ic: e.tensor_tensor(
                s_eact[:, i, :], s_es1, pg_u[:, :], OP.mult),
                reads=[b_es1, pbg_u], writes=[b_eact])
        for tb in range(4):
            for ic in range(4):
                plan.op("pe", lambda e, i=ic, t=tb: e.matmul(
                    pd0[:, 0:512], s_eact[:, i, t * 128:(t + 1) * 128], s_wd[:, i, 0:512],
                    start=(i == 0), stop=(i == 3)), reads=[b_eact, b_wd], writes=[pb_d0])
            for ic in range(4):
                plan.op("pe", lambda e, i=ic, t=tb: e.matmul(
                    pd0[:, 512:1024], s_eact[:, i, t * 128:(t + 1) * 128], s_wd[:, i, 512:1024],
                    start=(i == 0), stop=(i == 3)), reads=[b_eact, b_wd], writes=[pb_d0])
            plan.op("act", lambda e: e.copy(s_eout[:, 0:512], pd0[:, 0:512]),
                    reads=[pb_d0], writes=[b_eout])
            plan.op("act", lambda e: e.copy(s_eout[:, 512:1024], pd0[:, 512:1024]),
                    reads=[pb_d0], writes=[b_eout], strict=True)
            plan.op("sp", lambda e, x=ex, t=tb: e.dma_start(
                out=out_all[x * 512 + t * 128:x * 512 + (t + 1) * 128, :], in_=s_eout),
                reads=[b_eout], writes=[db(out_all, ex % 8)], dma=True)
        yield

    run_interleaved(gen_scan(), gen_experts())

    arA.reset()
    s_ciTt, b_ciT = small("s_ciTt", (128, 128), U32)
    s_ciT = s_ciTt[:, :]
    s_wpT, b_wpT = arA.carve_f32("s_wpT", (128, 128))
    s_tmpT, b_tmpT = arA.carve_f32("s_tmpT", (128, 128))
    s_crow = [arA.carve_bf16(f"s_crow{i}", (128, 1024)) for i in range(2)]
    s_wblkf, b_wblkf = arA.carve_f32("s_wblkf", (128, 16))
    s_wblk, b_wblk = arA.carve_bf16("s_wblk", (128, 16))
    s_ym, b_ym = arA.carve_f32("s_ym", (128, 1024))

    for half in range(2):
        plan.op("sp", lambda e, hh=half: e.dma_start(
            out=s_tmpT.bitcast(U32)[0:64, 0:128],
            in_=cidx_d[hh * 8192:(hh + 1) * 8192, :].rearrange("(g i) x -> g (i x)", g=64)),
            reads=[], extra_deps=db_all(cidx_d), writes=[b_tmpT], dma=True)
        pt, pbt = pk[0]
        plan.op("pe", lambda e, p=pt: e.transpose(p[:, 0:64], s_tmpT[0:64, 0:128], c_ident[0:64, 0:64]),
                reads=[b_tmpT, b_ident], writes=[pbt])
        plan.op("dve", lambda e, hh=half, p=pt: e.tensor_copy(
            s_ciT.bitcast(F32)[:, hh * 64:(hh + 1) * 64], p[:, 0:64]),
            reads=[pbt], writes=[b_ciT], strict=True)
        plan.op("sp", lambda e, hh=half: e.dma_start(
            out=s_tmpT[0:64, 0:128],
            in_=wpr_d[hh * 8192:(hh + 1) * 8192, :].rearrange("(g i) x -> g (i x)", g=64)),
            reads=[], extra_deps=db_all(wpr_d), writes=[b_tmpT], dma=True)
        plan.op("pe", lambda e, p=pt: e.transpose(p[:, 0:64], s_tmpT[0:64, 0:128], c_ident[0:64, 0:64]),
                reads=[b_tmpT, b_ident], writes=[pbt], strict=True)
        plan.op("dve", lambda e, hh=half, p=pt: e.tensor_copy(
            s_wpT[:, hh * 64:(hh + 1) * 64], p[:, 0:64]),
            reads=[pbt], writes=[b_wpT], strict=True)

    def gen_ph4a():
      for g in range(128):
        scr, bcr = s_crow[g % 2]
        plan.op("pool", lambda e, gg=g, d=scr: e.indirect_dma_start(
            d, None, out_all, IndirectOffsetOnAxis(s_ciT[:, gg:gg + 1], 0),
            bounds_check=bcreg(e, E * CAP - 1), oob_is_err=False),
            reads=[b_ciT], extra_deps=db_all(out_all), writes=[bcr], dma=True)
        plan.op("dve", lambda e, gg=g: e.tensor_scalar(
            s_wblkf, c_blk[:, :], s_wpT[:, gg:gg + 1], None, OP.mult),
            reads=[b_blk, b_wpT], writes=[b_wblkf])
        plan.op("dve", lambda e: e.tensor_copy(s_wblk, s_wblkf),
                reads=[b_wblkf], writes=[b_wblk], strict=True)
        pc, pbc = pk[1] if g % 2 == 0 else pk[2]
        pc2, pbc2 = pk[3]
        plan.op("pe", lambda e, d=scr, p=pc: e.matmul(
            p[0:16, 0:512], s_wblk[:, :], d[:, 0:512], start=True, stop=True),
            reads=[b_wblk, bcr], writes=[pbc])
        plan.op("pe", lambda e, d=scr, p=pc2: e.matmul(
            p[0:16, 0:512], s_wblk[:, :], d[:, 512:1024], start=True, stop=True),
            reads=[b_wblk, bcr], writes=[pbc2])
        plan.op("dve", lambda e, p=pc: e.tensor_copy(s_ym[0:16, 0:512], p[0:16, 0:512]),
                reads=[pbc], writes=[b_ym])
        plan.op("dve", lambda e, p=pc2: e.tensor_copy(s_ym[0:16, 512:1024], p[0:16, 0:512]),
                reads=[pbc2], writes=[b_ym], strict=True)
        plan.op("sp", lambda e, gg=g: e.dma_start(
            out=ymoe_d[gg * 16:(gg + 1) * 16, :], in_=s_ym[0:16, :]),
            reads=[b_ym], writes=[db(ymoe_d, g % 4)], dma=True)
        yield

    s_mxT, b_mxT = arA.carve_bf16("s_mxT", (128, 8, 512))
    s_mxw = [arA.carve_bf16(f"s_mxw{i}", (128, 8, 128)) for i in range(2)]
    s_qT, b_qT = arA.carve_bf16("s_qT", (128, L))
    s_kT, b_kT = arA.carve_bf16("s_kT", (128, L))
    s_aT, b_aT = arA.carve_bf16("s_aT", (128, L))
    s_bT, b_bT = arA.carve_bf16("s_bT", (128, L))
    s_bstage, b_bstage = arA.carve_f32("s_bstage", (128, 512))
    s_z1, b_z1 = arA.carve_f32("s_z1", (128, 512))
    s_sc, b_scb = arA.carve_f32("s_sc", (128, L))
    s_p32, b_p32 = arA.carve_f32("s_p32", (128, L))
    s_pbf, b_pbf = arA.carve_bf16("s_pbf", (128, L))
    s_atT, b_atT = arA.carve_bf16("s_atT", (128, 16, 128))
    s_vv = [arA.carve_bf16(f"s_vv{i}", (128, 1024)) for i in range(6)]
    s_hw, b_hw = arA.carve_bf16("s_hw", (128, 8, 1024))
    plan.op("sp", lambda e: e.dma_start(
        out=s_hw[:, :, :], in_=hwT_bf.rearrange("(c p) d -> p c d", p=128)),
        reads=[db(hwT_bf)], writes=[b_hw], dma=True)
    s_hro, b_hro = arA.carve_bf16("s_hro", (128, 8, 128))
    s_smcol, b_smcol = arA.carve_f32("s_smcol", (128, 4))
    arB.reset()
    s_csn, b_csn = arB.carve_f32("s_csn", (128, 512))
    s_sn2, b_sn2 = arB.carve_f32("s_sn2", (128, 512))
    s_t1b, b_t1b = arB.carve_f32("s_t1b", (128, 512))
    s_rre, b_rre = arB.carve_f32("s_rre", (128, 8, 128))
    s_hxe, b_hxe = arB.carve_f32("s_hxe", (128, 8, 128))
    s_ue, b_ue = arB.carve_f32("s_ue", (128, 8, 128))
    s_xne, b_xne = arB.carve_f32("s_xne", (128, 8, 128))
    s_she, b_she = arB.carve_f32("s_she", (128, 8, 128))
    s_e1, b_e1 = arB.carve_f32("s_e1", (128, 128))
    s_hf, b_hf = arB.carve_f32("s_hf", (128, 128))
    s_sumT, b_sumT = arB.carve_f32("s_sumT", (128, 8, 128))
    s_xo, b_xo = arB.carve_f32("s_xo", (128, 1024))
    s_ymo, b_ymo = arB.carve_f32("s_ymo", (128, 1024))
    s_outt, b_outt = arB.carve_f32("s_outt", (128, 1024))
    s_vv += [arB.carve_bf16(f"s_vv{i + 6}", (128, 1024)) for i in range(10)]
    for vb in range(16):
        vt, bvt = s_vv[vb]
        plan.op("sp", lambda e, s2=vb, d=vt: e.dma_start(
            out=d, in_=v_d[s2 * 128:(s2 + 1) * 128, :]),
            reads=[], extra_deps=db_all(v_d), writes=[bvt], dma=True)

    for cc in range(4):
        plan.op("sp", lambda e, c2=cc: e.dma_start(
            out=s_csn, in_=din["cos2"][:, c2 * 512:(c2 + 1) * 512]),
            writes=[b_csn], dma=True)
        plan.op("sp", lambda e, c2=cc: e.dma_start(
            out=s_sn2, in_=din["sin2"][:, c2 * 512:(c2 + 1) * 512]),
            writes=[b_sn2], dma=True)
        for (dst, bdst, c1, c2n) in [(s_aT, b_aT, "A1_col", "A2_col"),
                                     (s_bT, b_bT, "B1_col", "B2_col")]:
            plan.op("dve", lambda e, cn=c1: e.tensor_scalar(
                s_t1b, s_csn, cols[cn][0][:, 0:1], None, OP.mult),
                reads=[b_csn, cols[c1][1]], writes=[b_t1b])
            plan.op("dve", lambda e, cn=c2n, d=dst, c3=cc: e.scalar_tensor_tensor(
                d[:, c3 * 512:(c3 + 1) * 512], s_sn2, cols[cn][0][:, 0:1], s_t1b,
                OP.mult, OP.add),
                reads=[b_sn2, cols[c2n][1], b_t1b], writes=[bdst])

    dump_now("aTd", s_aT, b_aT, [128, L], BF16)
    dump_now("bTd", s_bT, b_bT, [128, L], BF16)
    def gen_ph4b():
      for tc in range(4):
        plan.op("sp", lambda e, c=tc: e.dma_start(
            out=s_mxT[:, :, :],
            in_=mxT_d.rearrange("(g p) t -> p g t", p=128)[:, :, c * 512:(c + 1) * 512]),
            reads=[db(mxT_d, tc)], writes=[b_mxT], dma=True)
        for fb in range(25):
            wmx, bwmx = s_mxw[fb % 2]
            plan.op("sp", lambda e, f=fb, w=wmx: e.dma_start(
                out=w[:, :, :], in_=mxwT_bf[:, f * 128:(f + 1) * 128]
                .rearrange("(c p) f -> p c f", p=128)),
                reads=[db(mxwT_bf)], writes=[bwmx], dma=True)
            pview = pd1[:, (fb % 2) * 512:(fb % 2) * 512 + 512]
            for hc in range(8):
                plan.op("pe", lambda e, h=hc, w=wmx, pv=pview: e.matmul(
                    pv, w[:, h, :], s_mxT[:, h, :],
                    start=(h == 0), stop=(h == 7)), reads=[bwmx, b_mxT], writes=[pb_d1])
            mb = cols["mx_b"]
            if fb < 8:
                plan.op("act", lambda e, f=fb, pv=pview: e.activation(
                    s_bstage, pv, AF.Sigmoid, bias=mb[0][:, f:f + 1]),
                    reads=[pb_d1, mb[1]], writes=[b_bstage])
                plan.op("sp", lambda e, f=fb, c=tc: e.dma_start(
                    out=uT_d[f * 128:(f + 1) * 128, c * 512:(c + 1) * 512], in_=s_bstage),
                    reads=[b_bstage], writes=[db(uT_d, (fb + tc) % 4)], dma=True)
            elif fb == 8:
                plan.op("act", lambda e, f=fb, pv=pview: e.activation(
                    s_z1, pv, AF.Silu, bias=mb[0][:, f:f + 1]),
                    reads=[pb_d1, mb[1]], writes=[b_z1])
                plan.op("dve", lambda e, c=tc: e.tensor_scalar(
                    s_qT[:, c * 512:(c + 1) * 512], s_z1, s_qg[:, 0:1], s_qb[:, 0:1],
                    OP.mult, OP.add), reads=[b_z1, b_qg, b_qb], writes=[b_qT])
                plan.op("dve", lambda e, c=tc: e.tensor_scalar(
                    s_kT[:, c * 512:(c + 1) * 512], s_z1,
                    cols["g1_col"][0][:, 0:1], cols["b1_col"][0][:, 0:1],
                    OP.mult, OP.add),
                    reads=[b_z1, cols["g1_col"][1], cols["b1_col"][1]], writes=[b_kT])
            elif fb < 17:
                plan.op("act", lambda e, f=fb, pv=pview: e.activation(
                    s_bstage, pv, AF.Silu, bias=mb[0][:, f:f + 1]),
                    reads=[pb_d1, mb[1]], writes=[b_bstage])
                plan.op("sp", lambda e, f=fb, c=tc: e.dma_start(
                    out=rT_d[(f - 9) * 128:(f - 8) * 128, c * 512:(c + 1) * 512],
                    in_=s_bstage), reads=[b_bstage], writes=[db(rT_d, (fb + tc) % 4)], dma=True)
            else:
                plan.op("act", lambda e, f=fb, pv=pview: e.activation(
                    s_bstage, pv, AF.Identity, bias=mb[0][:, f:f + 1]),
                    reads=[pb_d1, mb[1]], writes=[b_bstage])
                plan.op("sp", lambda e, f=fb, c=tc: e.dma_start(
                    out=hxT_d[(f - 17) * 128:(f - 16) * 128, c * 512:(c + 1) * 512],
                    in_=s_bstage), reads=[b_bstage], writes=[db(hxT_d, (fb + tc) % 4)], dma=True)
            yield tc * 32 + fb

        if tc == 0:
            dump_now("qTd", s_qT[:, 0:512], b_qT, [128, 512], BF16)
            dump_now("kTd", s_kT[:, 0:512], b_kT, [128, 512], BF16)
        for i in range(4):
            ib = tc * 4 + i
            yield 8 * (ib + 1)
            send = (ib + 1) * 128
            nsc = (send + 511) // 512
            for sc in range(nsc):
                w = min(512, send - sc * 512)
                pq, pbq = pk[sc % 2]
                plan.op("pe", lambda e, b2=ib, s2=sc, wd=w, p=pq: e.matmul(
                    p[:, 0:wd], s_qT[:, b2 * 128:(b2 + 1) * 128],
                    s_kT[:, s2 * 512:s2 * 512 + wd], start=True, stop=False),
                    reads=[b_qT, b_kT], writes=[pbq])
                plan.op("pe", lambda e, b2=ib, s2=sc, wd=w, p=pq: e.matmul(
                    p[:, 0:wd], s_aT[:, b2 * 128:(b2 + 1) * 128],
                    s_bT[:, s2 * 512:s2 * 512 + wd], start=False, stop=True),
                    reads=[b_aT, b_bT], writes=[pbq])
                if sc == nsc - 1:
                    if w > 128:
                        plan.op("dve", lambda e, s2=sc, wd=w, p=pq: e.tensor_copy(
                            s_sc[:, s2 * 512:s2 * 512 + wd - 128], p[:, 0:wd - 128]),
                            reads=[pbq], writes=[b_scb])
                    plan.op("dve", lambda e, s2=sc, wd=w, p=pq: e.tensor_tensor(
                        s_sc[:, s2 * 512 + wd - 128:s2 * 512 + wd],
                        p[:, wd - 128:wd], c_diag[:, :], OP.add),
                        reads=[pbq, b_diag], writes=[b_scb])
                else:
                    plan.op("dve", lambda e, s2=sc, wd=w, p=pq: e.tensor_copy(
                        s_sc[:, s2 * 512:s2 * 512 + wd], p[:, 0:wd]),
                        reads=[pbq], writes=[b_scb])
            if ib == 0:
                dump_now("scd", s_sc[:, 0:128], b_scb, [128, 128], F32)
            plan.op("dve", lambda e, se=send: e.reduce_max(
                s_smcol[:, 0:1], s_sc[:, 0:se], AX.X), reads=[b_scb], writes=[b_smcol])
            plan.op("dve", lambda e: e.tensor_scalar(
                s_smcol[:, 1:2], s_smcol[:, 0:1], -1.0, None, OP.mult),
                reads=[b_smcol], writes=[b_smcol], strict=True)
            plan.op("act", lambda e, se=send: e.activation(
                s_p32[:, 0:se], s_sc[:, 0:se], AF.Exp, bias=s_smcol[:, 1:2],
                accum_out=s_smcol[:, 2:3]),
                reads=[b_scb, b_smcol], writes=[b_p32, b_smcol])
            plan.op("dve", lambda e: e.reciprocal(s_smcol[:, 3:4], s_smcol[:, 2:3]),
                    reads=[b_smcol], writes=[b_smcol], strict=True)
            plan.op("act", lambda e, se=send: e.activation(
                s_pbf[:, 0:se], s_p32[:, 0:se], AF.Identity, scale=s_smcol[:, 3:4]),
                reads=[b_p32, b_smcol], writes=[b_pbf])
            if ib == 0:
                dump_now("pbfd", s_pbf[:, 0:128], b_pbf, [128, 128], BF16)
                dump_now("smcd", s_smcol, b_smcol, [128, 4], F32)
            for sb in range(ib + 1):
                plan.op("pe", lambda e, s2=sb: e.transpose(
                    pk[2][0][:, 0:64].bitcast(BF16),
                    s_pbf[:, s2 * 128:(s2 + 1) * 128], c_identb[:, :]),
                    reads=[b_pbf, b_identb], writes=[pk[2][1]])
                plan.op("dve", lambda e, s2=sb: e.tensor_copy(
                    s_atT[:, s2, :], pk[2][0][:, 0:64].bitcast(BF16)),
                    reads=[pk[2][1]], writes=[b_atT], strict=True)
            for d8 in range(8):
                for sb in range(ib + 1):
                    vt, bvt = s_vv[sb]
                    plan.op("pe", lambda e, s2=sb, dd=d8, v2=vt, b2=ib: e.matmul(
                        pd0[:, dd * 128:(dd + 1) * 128],
                        v2[:, dd * 128:(dd + 1) * 128], s_atT[:, s2, :],
                        start=(s2 == 0), stop=(s2 == b2)),
                        reads=[bvt, b_atT], writes=[pb_d0])
            if ib == 0 and "pavd" in debug_outs:
                s_pav, b_pav = small("s_pav", (128, 1024))
                for _d8 in range(2):
                    plan.op("act", lambda e, d=_d8: e.copy(
                        s_pav[:, d * 512:(d + 1) * 512], pd0[:, d * 512:(d + 1) * 512]),
                        reads=[pb_d0], writes=[b_pav])
                dump_now("pavd", s_pav[:, :], b_pav, [128, 1024], F32)
            plan.op("pool", lambda e, b2=ib: e.dma_start(
                out=s_rre[:, :, :],
                in_=rT_d.rearrange("(g p) t -> p g t", p=128)[:, :, b2 * 128:(b2 + 1) * 128]),
                reads=[], extra_deps=db_all(rT_d), writes=[b_rre], dma=True)
            for g8 in range(8):
                plan.op("dve", lambda e, g=g8: e.tensor_tensor(
                    s_hro[:, g, :], pd0[:, g * 128:(g + 1) * 128], s_rre[:, g, :], OP.mult),
                    reads=[pb_d0, b_rre], writes=[b_hro])
            plan.op("pool", lambda e, b2=ib: e.dma_start(
                out=s_hxe[:, :, :],
                in_=hxT_d.rearrange("(g p) t -> p g t", p=128)[:, :, b2 * 128:(b2 + 1) * 128]),
                reads=[], extra_deps=db_all(hxT_d), writes=[b_hxe], dma=True)
            plan.op("pool", lambda e, b2=ib: e.dma_start(
                out=s_ue[:, :, :],
                in_=uT_d.rearrange("(g p) t -> p g t", p=128)[:, :, b2 * 128:(b2 + 1) * 128]),
                reads=[], extra_deps=db_all(uT_d), writes=[b_ue], dma=True)
            plan.op("pool", lambda e, b2=ib: e.dma_start(
                out=s_xne[:, :, :],
                in_=xnT_d.rearrange("(g p) t -> p g t", p=128)[:, :, b2 * 128:(b2 + 1) * 128]),
                reads=[], extra_deps=db_all(xnT_d), writes=[b_xne], dma=True)
            plan.op("pool", lambda e, b2=ib: e.dma_start(
                out=s_she[:, :, :],
                in_=sharedT_d.rearrange("(g p) t -> p g t", p=128)[:, :, b2 * 128:(b2 + 1) * 128]),
                reads=[], extra_deps=db_all(sharedT_d), writes=[b_she], dma=True)
            plan.op("sp", lambda e, b2=ib: e.dma_start(
                out=s_xo, in_=x_own[b2 * 128:(b2 + 1) * 128, :]), writes=[b_xo], dma=True)
            plan.op("sp", lambda e, b2=ib: e.dma_start(
                out=s_ymo, in_=ymoe_d[b2 * 128:(b2 + 1) * 128, :]),
                reads=[], extra_deps=db_all(ymoe_d), writes=[b_ymo], dma=True)
            for j8 in range(8):
                pj, pbj = pk[3]
                for dc in range(8):
                    plan.op("pe", lambda e, d2=dc, j=j8, p=pj: e.matmul(
                        p[:, 0:128], s_hw[:, d2, j * 128:(j + 1) * 128], s_hro[:, d2, :],
                        start=(d2 == 0), stop=(d2 == 7)),
                        reads=[b_hw, b_hro], writes=[pbj])
                plan.op("dve", lambda e, j=j8, p=pj: e.tensor_tensor(
                    s_e1, p[:, 0:128], s_hxe[:, j, :], OP.add),
                    reads=[pbj, b_hxe], writes=[b_e1])
                plan.op("act", lambda e, j=j8: e.activation(
                    s_hf, s_e1, AF.Silu, bias=cols["h_b"][0][:, j:j + 1]),
                    reads=[b_e1, cols["h_b"][1]], writes=[b_hf])
                plan.op("dve", lambda e, j=j8: e.tensor_tensor(
                    s_e1, s_hf, s_xne[:, j, :], OP.subtract),
                    reads=[b_hf, b_xne], writes=[b_e1], strict=True)
                plan.op("dve", lambda e, j=j8: e.tensor_tensor(
                    s_e1, s_e1, s_ue[:, j, :], OP.mult),
                    reads=[b_e1, b_ue], writes=[b_e1], strict=True)
                plan.op("dve", lambda e, j=j8: e.tensor_tensor(
                    s_e1, s_e1, s_xne[:, j, :], OP.add),
                    reads=[b_e1, b_xne], writes=[b_e1], strict=True)
                plan.op("dve", lambda e, j=j8: e.tensor_tensor(
                    s_sumT[:, j, :], s_e1, s_she[:, j, :], OP.add),
                    reads=[b_e1, b_she], writes=[b_sumT])
            if ib == 0:
                dump_now("sumTd", s_sumT.rearrange("p j t -> p (j t)"), b_sumT,
                         [128, 1024], F32)
                dump_now("hrod", s_hro.rearrange("p g t -> p (g t)"), b_hro,
                         [128, 1024], BF16)
            for j8 in range(8):
                pt3, pbt3 = pk[2]
                plan.op("pe", lambda e, j=j8, p=pt3: e.transpose(
                    p[:, 0:128], s_sumT[:, j, :], c_ident[:, :]),
                    reads=[b_sumT, b_ident], writes=[pbt3])
                plan.op("dve", lambda e, j=j8, p=pt3: e.scalar_tensor_tensor(
                    s_outt[:, j * 128:(j + 1) * 128], p[:, 0:128], 1.0,
                    s_xo[:, j * 128:(j + 1) * 128], OP.bypass, OP.add),
                    reads=[pbt3, b_xo], writes=[b_outt])
                plan.op("dve", lambda e, j=j8: e.tensor_tensor(
                    s_outt[:, j * 128:(j + 1) * 128], s_outt[:, j * 128:(j + 1) * 128],
                    s_ymo[:, j * 128:(j + 1) * 128], OP.add),
                    reads=[b_ymo], writes=[b_outt], strict=True)
            if ib == 0:
                dump_now("outtd", s_outt, b_outt, [128, 1024], F32)
            plan.op("sp", lambda e, b2=ib: e.dma_start(
                out=y_out[b2 * 128:(b2 + 1) * 128, :], in_=s_outt),
                reads=[b_outt], dma=True)

    _ga = gen_ph4a()
    _na = [0]
    def _pump_a(n):
        while _na[0] < min(n, 128):
            try:
                next(_ga)
            except StopIteration:
                return
            _na[0] += 1
    for _req in gen_ph4b():
        _pump_a(_req)
    _pump_a(128)

    dbg = dict(_inline)
    if debug_outs:
        sbuf_dumps = {
            "dtin_sb": (s_dtin[:, :], b_dtin, [64, L], BF16),
            "cnt_sb": (s_cntall.rearrange("p a c -> p (a c)"), b_cntall, [128, 128 * 64], BF16),
            "idx_sb": (s_idxall.rearrange("p a c -> p (a c)"), b_idxall, [128, 128 * 8], U32),
            "lt_sb": (s_lt[0:64, :], b_ltb, [64, 512], F32),
            "toffsb": (s_toff[:, :] if False else s_toff, b_toff, [128, 64], F32),
        }
        for nm in [d for d in debug_outs if d in sbuf_dumps and d not in _inline]:
            ap2, b2, shp, dt2 = sbuf_dumps[nm]
            od = nc.dram_tensor("dbg_" + nm, shp, dt2, kind="ExternalOutput").ap()
            plan.op("sp", lambda e, s2=ap2, d2=od: e.dma_start(out=d2, in_=s2),
                    reads=[b2], dma=True)
            dbg[nm] = "dbg_" + nm
    if debug_outs:
        dumpable = dict(xn_all=xn_all, xnT_d=xnT_d, xn2T_d=xn2T_d, L_d=L_d,
                        mxT_d=mxT_d, v_d=v_d, sharedT_d=sharedT_d, uT_d=uT_d,
                        rT_d=rT_d, hxT_d=hxT_d, gtab_d=gtab_d, cidx_d=cidx_d,
                        wpr_d=wpr_d, out_all=out_all, ymoe_d=ymoe_d, toff_d=toff_d,
                        negB_d=negB_d, Cm_d=Cm_d, grow_d=grow_d, buf_d=buf_d)
        for nm in [d for d in debug_outs if d in dumpable]:
            t = dumpable[nm]
            od = nc.dram_tensor("dbg_" + nm, list(t.tensor.shape), t.dtype,
                                kind="ExternalOutput").ap()
            plan.op("sp", lambda e, s2=t, d2=od: e.dma_start(out=d2, in_=s2),
                    reads=[db(t)], dma=True)
            dbg[nm] = "dbg_" + nm
    plan.finalize(stk.enter_context(nc.Block()), sems, dsems)
    stk.close()
    return nc, dbg



_CACHE = {}


def kernel(**inputs):
    a = _host_arrays(inputs)
    a["ones_row_bf"] = np.ones((1, 128), ml_dtypes.bfloat16)
    cp = np.zeros((128, 8), np.float32)
    for i, v in enumerate([1e-5, -1.0, -0.5, 1.0 / D, 1.0, 0.0, 1e-38, 511.0]):
        cp[:, i] = v
    a["cpool"] = cp
    if "nc" not in _CACHE:
        _CACHE["nc"] = build_program()
    nc, dbg = _CACHE["nc"]
    in_maps = []
    for c in range(NCORES):
        m = dict(a)
        m["x_own"] = np.ascontiguousarray(a["x_full"][c * L:(c + 1) * L, :])
        sel = np.zeros((128, 128), np.float32)
        for j in range(16):
            sel[c * 16 + j, j] = 1.0
        m["sel16"] = sel
        in_maps.append(m)
    res = run_bass_kernel_spmd(nc, in_maps, core_ids=list(range(NCORES)))
    out = np.stack([res.results[c]["y"] for c in range(NCORES)], 0)
    return out.reshape(B, L, D).astype(np.float32)


if __name__ == "__main__":
    nc, dbg = build_program()
    tot = 0
    for e in ENGS:
        n = len([0 for _ in []])
    print("built ok")


# version 22
# speedup vs baseline: 1.0233x; 1.0233x over previous
import numpy as np
import ml_dtypes
import concourse.bass as bass
import concourse.mybir as mybir
from concourse.bass import IndirectOffsetOnAxis
from concourse.bass_utils import run_bass_kernel_spmd

F32 = mybir.dt.float32
BF16 = mybir.dt.bfloat16
U32 = mybir.dt.uint32
AF = mybir.ActivationFunctionType
OP = mybir.AluOpType
AX = mybir.AxisListType

NCORES = 8
B, L, D = 8, 2048, 1024
TA = B * L
ZDIM = 128
DTR = 64
NST = 16
E = 64
CAP = 512
TOPK = 8
ISH = 2048
IEXP = 512
MXF = 3200
EPS = 1e-5


ENGS = ("pe", "dve", "act", "pool", "sp")
STRICT_ALL = True
DEBUG_LINES = False


class Buf:
    __slots__ = ("name", "writer", "readers", "init_deps")

    def __init__(self, name, init_deps=()):
        self.name = name
        self.writer = None
        self.readers = []
        self.init_deps = list(init_deps)


class _Op:
    __slots__ = ("eng", "fn", "deps", "inc", "marked", "val", "strict", "selfwait",
                 "line")

    def __init__(self, eng, fn, inc, strict):
        import sys
        self.line = sys._getframe(3).f_lineno if DEBUG_LINES else 0
        self.eng = eng
        self.fn = fn
        self.deps = []
        self.inc = inc
        self.marked = inc >= 16
        self.val = None
        self.strict = strict
        self.selfwait = 0


class Plan:
    def __init__(self):
        self.ops = {e: [] for e in ENGS}
        self.order = []

    def op(self, eng, fn, reads=(), writes=(), dma=False, strict=False, extra_deps=()):
        o = _Op(eng, fn, 16 if dma else 1, strict or STRICT_ALL)
        for b in reads:
            if b.writer is not None:
                o.deps.append(b.writer)
            elif b.init_deps:
                o.deps.extend(b.init_deps)
        for b in writes:
            if b.writer is not None:
                o.deps.append(b.writer)
            elif b.init_deps:
                o.deps.extend(b.init_deps)
            o.deps.extend(b.readers)
        for d in extra_deps:
            if isinstance(d, Buf):
                if d.writer is not None:
                    o.deps.append(d.writer)
            elif d is not None:
                o.deps.append(d)
        for b in reads:
            b.readers.append(o)
        for b in writes:
            b.writer = o
            b.readers = []
        self.ops[eng].append(o)
        self.order.append(o)
        return o

    def finalize(self, block, sems, dsems):
        for o in self.order:
            for d in o.deps:
                if d.eng != o.eng or (o.strict and o.eng != "pe"):
                    d.marked = True
        semmap = {}
        for e2, h in sems.items():
            semmap[(e2,)] = h
        for e2, hl in dsems.items():
            for i, h in enumerate(hl):
                semmap[("d", e2, i)] = h
        cnt = {e: 0 for e in ENGS}
        dcnt = {e: 0 for e in ENGS}
        final = {}
        for e in ENGS:
            for o in self.ops[e]:
                if o.inc >= 16:
                    ns = len(dsems[e])
                    slot, rnd = dcnt[e] % ns, dcnt[e] // ns
                    o.val = (("d", e, slot), 16 * (rnd + 1))
                    o.selfwait = 16 * rnd
                    dcnt[e] += 1
                    final[o.val[0]] = o.val[1]
                elif o.marked:
                    cnt[e] += 1
                    o.val = ((e,), cnt[e])
                    final[(e,)] = cnt[e]

        def emit(e, name):
            water = {}
            for o in self.ops[name]:
                need = {}
                for d in o.deps:
                    if d.val is None:
                        continue
                    if (d.eng == name and d.inc < 16
                            and (not o.strict or name == "pe")):
                        continue
                    k, v = d.val
                    need[k] = max(need.get(k, 0), v)
                if o.inc >= 16 and o.selfwait > 0:
                    k = o.val[0]
                    need[k] = max(need.get(k, 0), o.selfwait)
                for k, v in sorted(need.items(), key=repr):
                    if v > water.get(k, 0):
                        e.wait_ge(semmap[k], v)
                        water[k] = v
                inst = o.fn(e)
                if o.val is not None:
                    if inst is None:
                        inst = e.engine_nop() if name in ("dve", "pool") else e.nop()
                    inst.then_inc(semmap[o.val[0]], o.inc)

        @block.tensor
        def _(e):
            emit(e, "pe")

        @block.vector
        def _(e):
            emit(e, "dve")

        @block.scalar
        def _(e):
            emit(e, "act")

        @block.gpsimd
        def _(e):
            emit(e, "pool")

        @block.sync
        def _(e):
            emit(e, "sp")
            for k, v in sorted(final.items(), key=repr):
                e.wait_ge(semmap[k], v)


class Arena:

    def __init__(self, plan, tensor, words):
        self.plan = plan
        self.t = tensor
        self.words = words
        self.off = 0
        self.live = []
        self.fence = []

    def reset(self):
        ops = []
        for b in self.live:
            if b.writer is not None:
                ops.append(b.writer)
            ops.extend(b.readers)
        self.fence = ops
        self.live = []
        self.off = 0

    def carve(self, name, words):
        aligned = (words + 15) & ~15
        assert self.off + aligned <= self.words, (
            f"arena overflow carving {name}: {self.off}+{aligned} > {self.words}")
        ap = self.t[:, self.off:self.off + words]
        self.off += aligned
        b = Buf(name, init_deps=self.fence)
        self.live.append(b)
        return ap, b

    def carve_f32(self, name, shape):
        fw = int(np.prod(shape[1:]))
        ap, b = self.carve(name, fw)
        if len(shape) == 3:
            ap = ap.rearrange("p (a c) -> p a c", a=shape[1])
        return ap, b

    def carve_bf16(self, name, shape):
        fw = int(np.prod(shape[1:]))
        ap, b = self.carve(name, (fw + 1) // 2)
        ap = ap.bitcast(BF16)
        if len(shape) == 3:
            ap = ap[:, :fw].rearrange("p (a c) -> p a c", a=shape[1])
        else:
            ap = ap[:, :fw]
        return ap, b

    def carve_u32(self, name, shape):
        fw = int(np.prod(shape[1:]))
        ap, b = self.carve(name, fw)
        ap = ap.bitcast(U32)
        if len(shape) == 3:
            ap = ap.rearrange("p (a c) -> p a c", a=shape[1])
        return ap, b



def _host_arrays(inputs):
    f32 = np.float32
    asc = np.ascontiguousarray
    a = {}
    a["x_full"] = asc(inputs["L0_attn_x"].reshape(TA, D).astype(f32, copy=False))
    a["norm_w"] = asc(np.asarray(inputs["L0_attn_norm_w"], f32).reshape(8, 128).T)
    a["v_wT"] = asc(np.asarray(inputs["L0_attn_v_w"], f32).T)
    a["v_b"] = asc(np.asarray(inputs["L0_attn_v_b"], f32).reshape(1, D))
    a["mx_wT"] = asc(np.asarray(inputs["L0_attn_mx_w"], f32).T)
    a["mx_b"] = asc(np.asarray(inputs["L0_attn_mx_b"], f32).reshape(MXF // 128, 128).T)
    a["h_wT"] = asc(np.asarray(inputs["L0_attn_h_w"], f32).T)
    a["h_b"] = asc(np.asarray(inputs["L0_attn_h_b"], f32).reshape(8, 128).T)
    g = np.asarray(inputs["L0_attn_gamma"], f32)
    bt = np.asarray(inputs["L0_attn_beta"], f32)
    a["g0_col"] = asc(g[0].reshape(128, 1)); a["g1_col"] = asc(g[1].reshape(128, 1))
    a["b0_col"] = asc(bt[0].reshape(128, 1)); a["b1_col"] = asc(bt[1].reshape(128, 1))
    al = np.asarray(inputs["L0_attn_alpha_rb"], f32).reshape(ZDIM)
    be = np.asarray(inputs["L0_attn_beta_rb"], f32).reshape(ZDIM)
    a["A1_col"] = asc(al.reshape(128, 1))
    a["A2_col"] = asc(np.concatenate([al[64:], al[:64]]).reshape(128, 1))
    a["B1_col"] = asc(be.reshape(128, 1))
    a["B2_col"] = asc(np.concatenate([be[64:], be[:64]]).reshape(128, 1))
    a["xproj_wT"] = asc(np.asarray(inputs["L0_attn_xproj_w"], f32).T)
    a["dt_wT"] = asc(np.asarray(inputs["L0_attn_dt_w"], f32).T)
    a["dt_b"] = asc(np.asarray(inputs["L0_attn_dt_b"], f32).reshape(8, 128).T)
    a["A_log"] = asc(np.asarray(inputs["L0_attn_A_log"], f32).reshape(8, 128, NST)
                     .transpose(1, 0, 2).reshape(128, 8 * NST))
    a["D_p"] = asc(np.asarray(inputs["L0_attn_D_p"], f32).reshape(8, 128).T)
    a["gate_w"] = asc(np.asarray(inputs["L0_ffn_gate_w"], f32))
    a["sg_w"] = asc(np.asarray(inputs["L0_ffn_shared_gate_w"], f32))
    a["sgu_w"] = asc(np.asarray(inputs["L0_ffn_shared_gate_up_w"], f32))
    a["sdown_w"] = asc(np.asarray(inputs["L0_ffn_shared_down_w"], f32))
    a["egu_w"] = asc(np.asarray(inputs["L0_ffn_expert_gate_up_w"], f32))
    a["edown_w"] = asc(np.asarray(inputs["L0_ffn_expert_down_w"], f32))
    a["identity"] = np.eye(128, dtype=f32)
    a["identity_bf"] = np.eye(128, dtype=ml_dtypes.bfloat16)
    a["ones_col_bf"] = np.ones((128, 1), ml_dtypes.bfloat16)
    a["tri_excl_bf"] = np.triu(np.ones((128, 128), f32), 1).astype(ml_dtypes.bfloat16)
    a["tri_excl_f32"] = np.triu(np.ones((128, 128), f32), 1)
    a["diagmask"] = np.triu(np.full((128, 128), -1e30, f32), 1)
    a["iota64_row"] = np.broadcast_to(np.arange(64, dtype=f32), (128, 64)).copy()
    a["iota128_col"] = np.arange(128, dtype=f32).reshape(128, 1)
    a["blkdiag"] = (np.arange(128)[:, None] // 8 == np.arange(16)[None, :]).astype(f32)
    half = ZDIM // 2
    freqs = np.exp(np.arange(half, dtype=np.float64) * (-np.log(10000.0) / half))
    ang = np.arange(L, dtype=np.float64)[:, None] * freqs[None, :]
    a["cos2"] = asc(np.concatenate([np.cos(ang).T, np.cos(ang).T], 0).astype(f32))
    a["sin2"] = asc(np.concatenate([-np.sin(ang).T, np.sin(ang).T], 0).astype(f32))
    return a



def build_program(debug_outs=()):
    nc = bass.Bass("TRN2", target_bir_lowering=False)
    from contextlib import ExitStack
    stk = ExitStack()
    plan = Plan()
    sems = {e: stk.enter_context(nc.semaphore(f"sem_{e}")) for e in ENGS}
    DSLOT = {"sp": 24, "pool": 16, "act": 6, "dve": 2, "pe": 2}
    dsems = {e: [stk.enter_context(nc.semaphore(f"dsem_{e}{i}"))
                 for i in range(DSLOT[e])] for e in ENGS}
    _inline = {}
    def dump_now(nm, ap2, b2, shp, dt2):
        if nm not in debug_outs:
            return
        od = nc.dram_tensor("dbg_" + nm, list(shp), dt2, kind="ExternalOutput").ap()
        plan.op("sp", lambda e: e.dma_start(out=od, in_=ap2), reads=[b2], dma=True)
        _inline[nm] = "dbg_" + nm
    _bregs = {}
    def bcreg(e, v):
        if v not in _bregs:
            _bregs[v] = e.to_reg(v)
        return _bregs[v]

    din = {}
    def dram_in(name, shape, dt=F32):
        din[name] = nc.dram_tensor(name, list(shape), dt, kind="ExternalInput").ap()
        return din[name]

    x_full = dram_in("x_full", (TA, D))
    x_own = dram_in("x_own", (L, D))
    dram_in("sel16", (128, 128))
    for nm, shp in [
        ("norm_w", (128, 8)), ("v_wT", (D, D)), ("v_b", (1, D)), ("mx_wT", (D, MXF)),
        ("mx_b", (128, MXF // 128)), ("h_wT", (D, D)), ("h_b", (128, 8)),
        ("g0_col", (128, 1)), ("g1_col", (128, 1)), ("b0_col", (128, 1)), ("b1_col", (128, 1)),
        ("A1_col", (128, 1)), ("A2_col", (128, 1)), ("B1_col", (128, 1)), ("B2_col", (128, 1)),
        ("xproj_wT", (D, 96)), ("dt_wT", (DTR, D)), ("dt_b", (128, 8)),
        ("A_log", (128, 8 * NST)), ("D_p", (128, 8)),
        ("gate_w", (D, E)), ("sg_w", (D, 1)), ("sgu_w", (D, 2 * ISH)), ("sdown_w", (ISH, D)),
        ("egu_w", (E, D, 2 * IEXP)), ("edown_w", (E, IEXP, D)),
        ("identity", (128, 128)), ("tri_excl_f32", (128, 128)),
        ("diagmask", (128, 128)), ("iota64_row", (128, 64)), ("iota128_col", (128, 1)),
        ("blkdiag", (128, 16)), ("cos2", (128, L)), ("sin2", (128, L)),
    ]:
        dram_in(nm, shp)
    dram_in("tri_excl_bf", (128, 128), BF16)
    dram_in("ones_col_bf", (128, 1), BF16)
    dram_in("identity_bf", (128, 128), BF16)
    dram_in("cpool", (128, 8))

    y_out = nc.dram_tensor("y", [L, D], F32, kind="ExternalOutput").ap()

    def dram_t(name, shape, dt=F32):
        return nc.dram_tensor(name, list(shape), dt).ap()

    xn_all = dram_t("xn_all", (TA, D), BF16)
    xn_own_d = dram_t("xn_own_d", (L, D), BF16)
    xnT_d = dram_t("xnT_d", (D, L))
    xn2T_d = dram_t("xn2T_d", (D, L))
    sharedT_d = dram_t("sharedT_d", (D, L))
    uT_d = dram_t("uT_d", (D, L))
    rT_d = dram_t("rT_d", (D, L))
    hxT_d = dram_t("hxT_d", (D, L))
    mxT_d = dram_t("mxT_d", (D, L), BF16)
    v_d = dram_t("v_d", (L, D), BF16)
    gtab_d = dram_t("gtab_d", (E * CAP, 1), U32)
    buf_d = dram_t("buf_d", (E * CAP, D), BF16)
    out_all = dram_t("out_all", (E * CAP, D), BF16)
    cidx_d = dram_t("cidx_d", (L * TOPK, 1), U32)
    wpr_d = dram_t("wpr_d", (L * TOPK, 1))
    ymoe_d = dram_t("ymoe_d", (L, D))
    toff_d = dram_t("toff_d", (128, E))
    toffown_d = dram_t("toffown_d", (16, E))
    tt_d = dram_t("tt_d", (1, 128 * E))
    negB_d = dram_t("negB_d", (NST, L))
    Cm_d = dram_t("Cm_d", (NST, L))
    grow_d = dram_t("grow_d", (1, L))
    sgu_bf = dram_t("sgu_bf", (D, 2 * ISH), BF16)
    sdown_bf = dram_t("sdown_bf", (ISH, D), BF16)
    vwT_bf = dram_t("vwT_bf", (D, D), BF16)
    mxwT_bf = dram_t("mxwT_bf", (D, MXF), BF16)
    hwT_bf = dram_t("hwT_bf", (D, D), BF16)

    dbuf = {}
    def db(t, r=None):
        key = (t.tensor.name, r)
        b = dbuf.get(key)
        if b is None:
            b = dbuf[key] = Buf(f"{key[0]}:{r}")
        return b
    def db_all(t):
        nm = t.tensor.name
        return [b for (n, _r), b in dbuf.items() if n == nm]

    pd0, pb_d0 = stk.enter_context(nc.psum_tensor("pd0", [128, 1024], F32)), Buf("pd0")
    pd1, pb_d1 = stk.enter_context(nc.psum_tensor("pd1", [128, 1024], F32)), Buf("pd1")
    pk = []
    for i in range(4):
        pk.append((stk.enter_context(nc.psum_tensor(f"pk{i}", [128, 512], F32)), Buf(f"pk{i}")))

    arA_t = stk.enter_context(nc.sbuf_tensor("arA", [128, 24576], F32))
    arB_t = stk.enter_context(nc.sbuf_tensor("arB", [128, 16384], F32))
    arA = Arena(plan, arA_t, 24576)
    arB = Arena(plan, arB_t, 16384)

    smalls = {}
    def small(name, shape, dt=F32):
        t = stk.enter_context(nc.sbuf_tensor(name, list(shape), dt))
        b = Buf(name)
        smalls[name] = (t, b)
        return t, b

    def ld(eng, dst_ap, dst_b, src_ap):
        plan.op(eng, lambda e: e.dma_start(out=dst_ap, in_=src_ap), writes=[dst_b], dma=True)

    c_ident, b_ident = small("c_ident", (128, 128))
    c_tri_bf, b_tri_bf = small("c_tri_bf", (128, 128), BF16)
    c_tri_f, b_tri_f = small("c_tri_f", (128, 128))
    c_diag, b_diag = small("c_diag", (128, 128))
    c_iota64, b_iota64 = small("c_iota64", (128, 64))
    c_iotac, b_iotac = small("c_iotac", (128, 1))
    c_blk, b_blk = small("c_blk", (128, 16))
    c_ones_bf, b_ones_bf = small("c_ones_bf", (128, 1), BF16)
    c_sel, b_sel = small("c_sel", (128, 128))
    c_identb, b_identb = small("c_identb", (128, 128), BF16)
    c_cpool, b_cpool = small("c_cpool", (128, 8))
    ld("sp", c_ident[:, :], b_ident, din["identity"])
    ld("sp", c_tri_bf[:, :], b_tri_bf, din["tri_excl_bf"])
    ld("sp", c_tri_f[:, :], b_tri_f, din["tri_excl_f32"])
    ld("sp", c_diag[:, :], b_diag, din["diagmask"])
    ld("sp", c_iota64[:, :], b_iota64, din["iota64_row"])
    ld("sp", c_iotac[:, :], b_iotac, din["iota128_col"])
    ld("sp", c_blk[:, :], b_blk, din["blkdiag"])
    ld("sp", c_ones_bf[:, :], b_ones_bf, din["ones_col_bf"])
    ld("sp", c_sel[:, :], b_sel, din["sel16"])
    ld("sp", c_identb[:, :], b_identb, din["identity_bf"])
    ld("sp", c_cpool[:, :], b_cpool, din["cpool"])
    _cvals = [1e-5, -1.0, -0.5, 1.0 / D, 1.0, 0.0, 1e-38, 511.0]
    for _i, _v in enumerate(_cvals):
        nc.const_aps.register(F32, _v, c_cpool[:, _i:_i + 1])
    for _e in ENGS:
        plan.op(_e, lambda e: None, reads=[b_cpool])

    w_gate, b_wgate = small("w_gate", (128, 8, E), BF16)
    plan.op("pool", lambda e: e.dma_start(
        out=w_gate[:, :, :], in_=din["gate_w"].rearrange("(hc p) e -> p hc e", p=128)),
        writes=[b_wgate], dma=True)
    w_xp, b_wxp = small("w_xp", (128, 8, 96), BF16)
    plan.op("pool", lambda e: e.dma_start(
        out=w_xp[:, :, :], in_=din["xproj_wT"].rearrange("(hc p) f -> p hc f", p=128)),
        writes=[b_wxp], dma=True)
    w_dt, b_wdt = small("w_dt", (64, 8, 128), BF16)
    plan.op("pool", lambda e: e.dma_start(
        out=w_dt[:, :, :], in_=din["dt_wT"].rearrange("r (c f) -> r c f", c=8)),
        writes=[b_wdt], dma=True)
    w_sg, b_wsg = small("w_sg", (128, 8, 1), BF16)
    plan.op("pool", lambda e: e.dma_start(
        out=w_sg[:, :, :], in_=din["sg_w"].rearrange("(hc p) o -> p hc o", p=128),
        allow_slow_non_contiguous=True),
        writes=[b_wsg], dma=True)
    w_vb, b_wvb = small("w_vb", (1, D), BF16)
    plan.op("pool", lambda e: e.dma_start(out=w_vb[:, :], in_=din["v_b"]),
            writes=[b_wvb], dma=True)
    s_dtin, b_dtin = small("s_dtin", (64, L), BF16)

    cols = {}
    for nm in ["g0_col", "g1_col", "b0_col", "b1_col", "A1_col", "A2_col", "B1_col",
               "B2_col", "dt_b", "h_b", "D_p", "A_log", "mx_b"]:
        shp = {"dt_b": (128, 8), "h_b": (128, 8), "D_p": (128, 8),
               "A_log": (128, 8 * NST), "mx_b": (128, MXF // 128)}.get(nm, (128, 1))
        t, b = small("s_" + nm, shp)
        cols[nm] = (t, b)
        ld("sp", t[tuple(slice(None) for _ in shp)], b, din[nm])
    t, b = small("s_norm_w", (128, 8))
    cols["norm_w"] = (t, b)
    ld("sp", t[:, :], b, din["norm_w"])

    s_expA, b_expA = small("s_expA", (128, 8 * NST))
    plan.op("act", lambda e: e.activation(s_expA[:, :], cols["A_log"][0][:, :], AF.Exp),
            reads=[cols["A_log"][1]], writes=[b_expA])
    s_ndtb, b_ndtb = small("s_ndtb", (128, 8))
    plan.op("dve", lambda e: e.tensor_scalar(s_ndtb[:, :], cols["dt_b"][0][:, :], -1.0, None, OP.mult),
            reads=[cols["dt_b"][1]], writes=[b_ndtb])
    s_qg, b_qg = small("s_qg", (128, 1))
    s_qb, b_qb = small("s_qb", (128, 1))
    sc128 = float(ZDIM) ** -0.5
    plan.op("dve", lambda e: e.tensor_scalar(s_qg[:, :], cols["g0_col"][0][:, :], sc128, None, OP.mult),
            reads=[cols["g0_col"][1]], writes=[b_qg])
    plan.op("dve", lambda e: e.tensor_scalar(s_qb[:, :], cols["b0_col"][0][:, :], sc128, None, OP.mult),
            reads=[cols["b0_col"][1]], writes=[b_qb])

    for srcn, dstt in [("sgu_w", sgu_bf), ("sdown_w", sdown_bf), ("v_wT", vwT_bf),
                       ("mx_wT", mxwT_bf), ("h_wT", hwT_bf)]:
        plan.op("pool", lambda e, s=srcn, d=dstt: e.dma_start(out=d, in_=din[s]),
                writes=[db(dstt)], dma=True)

    dbg = {}
    def dbg_dump(name, ap, shape, dt=F32, deps=()):
        if name not in debug_outs:
            return
        od = nc.dram_tensor("dbg_" + name, list(shape), dt, kind="ExternalOutput").ap()
        plan.op("sp", lambda e: e.dma_start(out=od, in_=ap), dma=True,
                extra_deps=list(deps))
        dbg[name] = od

    arA.reset(); arB.reset()
    invD = 1.0 / D
    s_x = []
    s_xnb = []
    for i in range(4):
        s_x.append(arA.carve_f32(f"s_x{i}", (128, D)))
    for i in range(2):
        s_xnb.append(arA.carve_bf16(f"s_xnb{i}", (128, D)))
    s_sq2 = [arA.carve_f32(f"s_sq{i}", (128, D)) for i in range(2)]
    s_lncol2 = [arA.carve_f32(f"s_lncol{i}", (128, 8)) for i in range(2)]
    s_xnf, b_xnf = arA.carve_f32("s_xnf", (128, D))
    s_x2s, b_x2s = arA.carve_f32("s_x2s", (128, D))
    s_tst, b_tst = arA.carve_f32("s_tst", (128, D))
    s_t2st, b_t2st = arA.carve_f32("s_t2st", (128, D))

    def emit_ln_tile(xsrc, xdst, k, own, xdst_b):
        sx, bx = s_x[k % 4]
        sxnb, bxnb = s_xnb[k % 2]
        s_sq, b_sq = s_sq2[k % 2]
        s_lncol, b_lncol = s_lncol2[k % 2]
        plan.op("sp", lambda e: e.dma_start(out=sx, in_=xsrc[k * 128:(k + 1) * 128, :]),
                writes=[bx], dma=True)
        plan.op("dve", lambda e: e.reduce_sum(s_lncol[:, 0:1], sx, AX.X),
                reads=[bx], writes=[b_lncol])
        plan.op("act", lambda e: e.activation(s_sq, sx, AF.Square, accum_out=s_lncol[:, 1:2]),
                reads=[bx], writes=[b_sq, b_lncol])
        plan.op("dve", lambda e: e.tensor_scalar(s_lncol[:, 2:3], s_lncol[:, 0:1], invD, None, OP.mult),
                reads=[b_lncol], writes=[b_lncol])
        plan.op("dve", lambda e: e.scalar_tensor_tensor(
            s_lncol[:, 3:4], s_lncol[:, 1:2], invD, s_lncol[:, 2:3], OP.mult, OP.bypass),
            reads=[b_lncol], writes=[b_lncol])
        plan.op("dve", lambda e: e.scalar_tensor_tensor(
            s_lncol[:, 3:4], s_lncol[:, 2:3], s_lncol[:, 2:3], s_lncol[:, 3:4],
            OP.mult, OP.subtract), reads=[b_lncol], writes=[b_lncol], strict=True)
        plan.op("act", lambda e: e.activation(s_lncol[:, 4:5], s_lncol[:, 3:4], AF.Ln,
                                              bias=EPS, scale=-1.0),
                reads=[b_lncol], writes=[b_lncol])
        plan.op("act", lambda e: e.activation(s_lncol[:, 4:5], s_lncol[:, 4:5], AF.Exp,
                                              scale=-0.5),
                reads=[b_lncol], writes=[b_lncol], strict=True)
        plan.op("dve", lambda e: e.scalar_tensor_tensor(
            s_lncol[:, 5:6], s_lncol[:, 2:3], -1.0, s_lncol[:, 4:5], OP.mult, OP.mult),
            reads=[b_lncol], writes=[b_lncol])
        plan.op("act", lambda e: e.activation(sxnb, sx, AF.Identity,
                                              bias=s_lncol[:, 5:6], scale=s_lncol[:, 4:5]),
                reads=[bx, b_lncol], writes=[bxnb])
        plan.op("pool", lambda e: e.dma_start(out=xdst[k * 128:(k + 1) * 128, :], in_=sxnb),
                reads=[bxnb], writes=[xdst_b], dma=True)
        if not own:
            return
        plan.op("act", lambda e: e.activation(s_xnf, sx, AF.Identity,
                                              bias=s_lncol[:, 5:6], scale=s_lncol[:, 4:5]),
                reads=[bx, b_lncol], writes=[b_xnf])
        plan.op("act", lambda e: e.activation(s_sq, s_xnf, AF.Square, accum_out=s_lncol[:, 6:7]),
                reads=[b_xnf], writes=[b_sq, b_lncol])
        plan.op("act", lambda e: e.activation(s_lncol[:, 6:7], s_lncol[:, 6:7], AF.Ln,
                                              bias=EPS, scale=invD),
                reads=[b_lncol], writes=[b_lncol], strict=True)
        plan.op("act", lambda e: e.activation(s_lncol[:, 6:7], s_lncol[:, 6:7], AF.Exp,
                                              scale=-0.5),
                reads=[b_lncol], writes=[b_lncol], strict=True)
        plan.op("act", lambda e: e.activation(s_x2s, s_xnf, AF.Identity, scale=s_lncol[:, 6:7]),
                reads=[b_xnf, b_lncol], writes=[b_x2s])
        for hc in range(8):
            pt, pbt = pk[hc % 2]
            plan.op("pe", lambda e, h=hc, p=pt: e.transpose(
                p[:, 0:128], s_xnf[:, h * 128:(h + 1) * 128], c_ident[:, :]),
                reads=[b_xnf, b_ident], writes=[pbt])
            plan.op("act", lambda e, h=hc, p=pt: e.copy(s_tst[:, h * 128:(h + 1) * 128], p[:, 0:128]),
                    reads=[pbt], writes=[b_tst])
            pt, pbt = pk[2 + hc % 2]
            plan.op("pe", lambda e, h=hc, p=pt: e.transpose(
                p[:, 0:128], s_x2s[:, h * 128:(h + 1) * 128], c_ident[:, :]),
                reads=[b_x2s, b_ident], writes=[pbt])
            plan.op("dve", lambda e, h=hc, p=pt: e.tensor_scalar(
                s_t2st[:, h * 128:(h + 1) * 128], p[:, 0:128],
                cols["norm_w"][0][:, h:h + 1], None, OP.mult),
                reads=[pbt, cols["norm_w"][1]], writes=[b_t2st])
        for hc in range(8):
            plan.op("sp", lambda e, h=hc: e.dma_start(
                out=xnT_d[h * 128:(h + 1) * 128, k * 128:(k + 1) * 128],
                in_=s_tst[:, h * 128:(h + 1) * 128]), reads=[b_tst], writes=[db(xnT_d, hc)], dma=True)
            plan.op("sp", lambda e, h=hc: e.dma_start(
                out=xn2T_d[h * 128:(h + 1) * 128, k * 128:(k + 1) * 128],
                in_=s_t2st[:, h * 128:(h + 1) * 128]), reads=[b_t2st], writes=[db(xn2T_d, hc)], dma=True)

    s_xnTt = []
    for i in range(2):
        ap, _b = arA.carve_bf16(f"s_xnTt{i}", (128, 8, 512))
        s_xnTt.append((ap, [Buf(f"s_xnTt{i}_h{h}", init_deps=_b.init_deps)
                            for h in range(8)]))
    s_lt, b_ltb = arA.carve_f32("s_lt", (128, 512))
    RT = []
    for _i in range(2):
        _d = {}
        _d["lg"], _d["b_lg"] = arA.carve_f32(f"s_lg{_i}", (128, 64))
        _d["m8"], _d["b_m8"] = arA.carve_f32(f"s_m8{_i}", (128, 8))
        _t, _b = small(f"s_i8t{_i}", (128, 8), U32)
        _d["i8"], _d["b_i8"] = _t[:, :], _b
        _d["idxf"], _d["b_idxf"] = arA.carve_f32(f"s_idxf{_i}", (128, 8))
        _d["cnta"], _d["b_cnta"] = arA.carve_f32(f"s_cnta{_i}", (128, 64))
        _d["cntb"], _d["b_cntb"] = arA.carve_f32(f"s_cntb{_i}", (128, 64))
        RT.append(_d)
    s_cntallt, b_cntall = small("s_cntallt", (128, 128, 64), BF16)
    s_cntall = s_cntallt[:, :, :]
    s_idxallt, b_idxall = small("s_idxallt", (128, 128, 8), U32)
    s_idxall = s_idxallt[:, :, :]

    def routing_tile_common(k_tile, cnt_store_ap, idx_store_ap, logits_psum_view, pb):
        R = RT[k_tile % 2]
        plan.op("dve", lambda e: e.tensor_copy(R["lg"], logits_psum_view),
                reads=[pb], writes=[R["b_lg"]])
        plan.op("dve", lambda e: e.max(R["m8"], R["lg"]),
                reads=[R["b_lg"]], writes=[R["b_m8"]])
        plan.op("dve", lambda e: e.max_index(R["i8"], R["m8"], R["lg"]),
                reads=[R["b_m8"], R["b_lg"]], writes=[R["b_i8"]], strict=True)
        plan.op("dve", lambda e: e.tensor_copy(R["idxf"], R["i8"]),
                reads=[R["b_i8"]], writes=[R["b_idxf"]])
        cur = (R["cnta"], R["b_cnta"])
        plan.op("dve", lambda e: e.tensor_scalar(
            R["cnta"], c_iota64[:, :], R["idxf"][:, 0:1], None, OP.is_equal),
            reads=[b_iota64, R["b_idxf"]], writes=[R["b_cnta"]])
        for j in range(1, 8):
            nxt = (R["cntb"], R["b_cntb"]) if j % 2 == 1 else (R["cnta"], R["b_cnta"])
            plan.op("dve", lambda e, jj=j, c=cur, n=nxt: e.scalar_tensor_tensor(
                n[0], c_iota64[:, :], R["idxf"][:, jj:jj + 1], c[0], OP.is_equal, OP.add),
                reads=[b_iota64, R["b_idxf"], cur[1]], writes=[nxt[1]])
            cur = nxt
        if k_tile == 0 and cnt_store_ap is not None:
            dump_now("idxf0", R["idxf"], R["b_idxf"], [128, 8], F32)
            dump_now("cntf0", cur[0], cur[1], [128, 64], F32)
        if cnt_store_ap is not None:
            plan.op("dve", lambda e: e.tensor_copy(cnt_store_ap, cur[0]),
                    reads=[cur[1]], writes=[b_cntall])
        if idx_store_ap is not None:
            plan.op("dve", lambda e: e.tensor_copy(idx_store_ap, R["i8"]),
                    reads=[R["b_i8"]], writes=[b_idxall])
        return cur, R

    def emit_logits_chunk(src_dram, tc, store_base_tile):
        sx, bxl = s_xnTt[tc % 2]
        for hc in range(8):
            plan.op("act", lambda e, h=hc: e.dma_start(
                out=sx[:, h, :],
                in_=src_dram[tc * 512:(tc + 1) * 512, h * 128:(h + 1) * 128],
                transpose=True), writes=[bxl[hc]], dma=True,
                extra_deps=db_all(src_dram))
        pl, pbl = pk[0]
        for hc in range(8):
            plan.op("pe", lambda e, h=hc: e.matmul(
                pl[0:64, :], w_gate[:, h, :], sx[:, h, :],
                start=(h == 0), stop=(h == 7)),
                reads=[b_wgate, bxl[hc]], writes=[pbl])
        plan.op("act", lambda e: e.copy(s_lt[0:64, :], pl[0:64, :]),
                reads=[pbl], writes=[b_ltb])
        for i in range(4):
            pt, pbt = pk[1]
            plan.op("pe", lambda e, ii=i: e.transpose(
                pt[:, 0:64], s_lt[0:64, ii * 128:(ii + 1) * 128], c_ident[0:64, 0:64]),
                reads=[b_ltb, b_ident], writes=[pbt])
            yield tc * 4 + i, pt[:, 0:64], pbt

    for k in range(TA // 128):
        emit_ln_tile(x_full, xn_all, k, False, db(xn_all, k % 8))
        if k % 4 == 3:
            tcc = k // 4
            for k_tile, pview, pb in emit_logits_chunk(xn_all, tcc, 0):
                routing_tile_common(k_tile, s_cntall[:, k_tile, :],
                                    s_idxall[:, k_tile, :], pview, pb)
    for k in range(L // 128):
        emit_ln_tile(x_own, xn_own_d, k, True, db(xn_own_d))

    dump_now("cnt_sb", s_cntall.rearrange("p a c -> p (a c)"), b_cntall, [128, 128 * 64], BF16)
    dump_now("idx_sb", s_idxall.rearrange("p a c -> p (a c)"), b_idxall, [128, 128 * 8], U32)

    s_ttrow, b_ttrow = arA.carve_f32("s_ttrow", (128, 512))
    cview = s_cntall.rearrange("p a c -> p (a c)")
    for i in range(16):
        pt, pbt = pk[2]
        plan.op("pe", lambda e, ii=i, p=pt: e.matmul(
            p[0:1, :], c_ones_bf[:, :], cview[:, ii * 512:(ii + 1) * 512],
            start=True, stop=True), reads=[b_ones_bf, b_cntall], writes=[pbt])
        plan.op("dve", lambda e, ii=i, p=pt: e.tensor_copy(s_ttrow[0:1, :], p[0:1, :]),
                reads=[pbt], writes=[b_ttrow])
        plan.op("sp", lambda e, ii=i: e.dma_start(
            out=tt_d[0:1, ii * 512:(ii + 1) * 512], in_=s_ttrow[0:1, :]),
            reads=[b_ttrow], writes=[db(tt_d)], dma=True)
    s_tt, b_tt = arA.carve_f32("s_tt", (128, 64))
    plan.op("sp", lambda e: e.dma_start(
        out=s_tt, in_=tt_d.rearrange("x (k c) -> (x k) c", k=128)),
        reads=[db(tt_d)], writes=[b_tt], dma=True)
    s_toff, b_toff = arA.carve_f32("s_toff", (128, 64))
    pt, pbt = pk[3]
    plan.op("pe", lambda e, p=pt: e.matmul(p[:, 0:64], c_tri_f[:, :], s_tt, start=True, stop=True),
            reads=[b_tri_f, b_tt], writes=[pbt])
    plan.op("dve", lambda e, p=pt: e.tensor_copy(s_toff, p[:, 0:64]), reads=[pbt], writes=[b_toff])
    plan.op("sp", lambda e: e.dma_start(out=toff_d, in_=s_toff),
            reads=[b_toff], writes=[db(toff_d)], dma=True)
    plan.op("pe", lambda e, p=pt: e.matmul(p[:, 64:128], c_sel[:, :], s_toff, start=True, stop=True),
            reads=[b_sel, b_toff], writes=[pbt])
    s_toffo, b_toffo = arA.carve_f32("s_toffo", (128, 64))
    plan.op("dve", lambda e, p=pt: e.tensor_copy(s_toffo[0:16, :], p[0:16, 64:128]),
            reads=[pbt], writes=[b_toffo])
    plan.op("sp", lambda e: e.dma_start(out=toffown_d, in_=s_toffo[0:16, :]),
            reads=[b_toffo], writes=[db(toffown_d)], dma=True)

    PB = []
    for _i in range(2):
        _d = {}
        for _nm, _shp in [("toffb", (128, 64)), ("pf", (128, 64)),
                          ("sc1", (128, 64)), ("pos", (128, 8)),
                          ("keep", (128, 8)), ("gx1", (128, 8)),
                          ("gcf", (128, 1)), ("idxf", (128, 8))]:
            _t, _b = small(f"s_pb_{_nm}{_i}", _shp)
            _d[_nm], _d["b_" + _nm] = _t[tuple(slice(None) for _ in _shp)], _b
        for _nm, _shp in [("gidx", (128, 8)), ("gcu", (128, 1))]:
            _t, _b = small(f"s_pb_{_nm}{_i}", _shp, U32)
            _d[_nm], _d["b_" + _nm] = _t[tuple(slice(None) for _ in _shp)], _b
        PB.append(_d)

    def emit_pos(k_tile, cnt_src_ap, b_cntsrc, b_idxf_src, idxf_ap, toff_dram,
                 toff_row, P):
        plan.op("sp", lambda e, st=P["toffb"]: e.dma_start(
            out=st, in_=toff_dram[toff_row:toff_row + 1, :].partition_broadcast(128)),
            writes=[P["b_toffb"]], dma=True, extra_deps=[db(toff_dram)])
        pp, pbp = pk[2]
        plan.op("pe", lambda e: e.matmul(pp[:, 0:64], c_tri_bf[:, :], cnt_src_ap,
                                         start=True, stop=True),
                reads=[b_tri_bf, b_cntsrc], writes=[pbp])
        plan.op("dve", lambda e, P2=P: e.tensor_tensor(
            P2["pf"], pp[:, 0:64], P2["toffb"], OP.add),
            reads=[pbp, P["b_toffb"]], writes=[P["b_pf"]])
        for j in range(8):
            plan.op("dve", lambda e, jj=j, P2=P, ix=idxf_ap: e.scalar_tensor_tensor(
                P2["sc1"], c_iota64[:, :], ix[:, jj:jj + 1], P2["pf"],
                OP.is_equal, OP.mult, accum_out=P2["pos"][:, jj:jj + 1]),
                reads=[b_iota64, b_idxf_src, P["b_pf"]],
                writes=[P["b_sc1"], P["b_pos"]])

    def gen_passB():
        for k in range(128):
            P = PB[k % 2]
            plan.op("dve", lambda e, kk=k, P2=P: e.tensor_copy(
                P2["idxf"], s_idxall[:, kk, :]),
                reads=[b_idxall], writes=[P["b_idxf"]])
            emit_pos(k, s_cntall[:, k, :], b_cntall, P["b_idxf"], P["idxf"],
                     toff_d, k, P)
            if k == 0:
                dump_now("pf0", P["pf"], P["b_pf"], [128, 64], F32)
                dump_now("pos0", P["pos"], P["b_pos"], [128, 8], F32)
            plan.op("dve", lambda e, P2=P: e.tensor_scalar(
                P2["keep"], P2["pos"], 512.0, None, OP.is_lt),
                reads=[P["b_pos"]], writes=[P["b_keep"]])
            plan.op("dve", lambda e, P2=P: e.scalar_tensor_tensor(
                P2["gx1"], P2["idxf"], 512.0, P2["pos"], OP.mult, OP.add),
                reads=[P["b_idxf"], P["b_pos"]], writes=[P["b_gx1"]])
            plan.op("dve", lambda e, P2=P: e.tensor_scalar(
                P2["gx1"], P2["gx1"], -70000.0, None, OP.add),
                reads=[P["b_gx1"]], writes=[P["b_gx1"]], strict=True)
            plan.op("dve", lambda e, P2=P: e.scalar_tensor_tensor(
                P2["gx1"], P2["keep"], 1.0, P2["gx1"], OP.bypass, OP.mult),
                reads=[P["b_keep"], P["b_gx1"]], writes=[P["b_gx1"]], strict=True)
            plan.op("dve", lambda e, P2=P: e.tensor_scalar(
                P2["gx1"], P2["gx1"], 70000.0, None, OP.add),
                reads=[P["b_gx1"]], writes=[P["b_gx1"]], strict=True)
            plan.op("dve", lambda e, P2=P: e.tensor_copy(P2["gidx"], P2["gx1"]),
                    reads=[P["b_gx1"]], writes=[P["b_gidx"]])
            plan.op("dve", lambda e, kk=k, P2=P: e.tensor_scalar(
                P2["gcf"], c_iotac[:, :], float(kk * 128), None, OP.add),
                reads=[b_iotac], writes=[P["b_gcf"]])
            plan.op("dve", lambda e, P2=P: e.tensor_copy(P2["gcu"], P2["gcf"]),
                    reads=[P["b_gcf"]], writes=[P["b_gcu"]])
            for j in range(8):
                plan.op("pool", lambda e, jj=j, P2=P: e.indirect_dma_start(
                    gtab_d, IndirectOffsetOnAxis(P2["gidx"][:, jj:jj + 1], 0),
                    P2["gcu"], None, bounds_check=bcreg(e, E * CAP - 1),
                    oob_is_err=False),
                    reads=[P["b_gidx"], P["b_gcu"]], writes=[db(gtab_d, j)],
                    dma=True)
            yield

    s_cob, b_cob = arA.carve_bf16("s_cob", (128, 64))
    s_wsum, b_wsum = arA.carve_f32("s_wsum", (128, 1))
    s_nm0, b_nm0 = arA.carve_f32("s_nm0", (128, 1))
    s_em8, b_em8 = arA.carve_f32("s_em8", (128, 8))
    s_w8, b_w8 = arA.carve_f32("s_w8", (128, 8))
    s_slot, b_slot = arA.carve_f32("s_slot", (128, 8))
    s_cif, b_cif = arA.carve_f32("s_cif", (128, 8))
    s_ciut, b_ciu = small("s_ciut", (128, 8), U32)
    s_ciu = s_ciut[:, :]

    for tc in range(4):
        for k_tile, pview, pb in emit_logits_chunk(xn_own_d, tc, 0):
            cur, R = routing_tile_common(k_tile, None, None, pview, pb)
            P = PB[k_tile % 2]
            plan.op("dve", lambda e, c=cur: e.tensor_copy(s_cob, c[0]),
                    reads=[cur[1]], writes=[b_cob])
            emit_pos(k_tile, s_cob, b_cob, R["b_idxf"], R["idxf"],
                     toffown_d, k_tile, P)
            plan.op("dve", lambda e, R2=R: e.tensor_scalar(
                s_nm0, R2["m8"][:, 0:1], -1.0, None, OP.mult),
                reads=[R["b_m8"]], writes=[b_nm0])
            plan.op("act", lambda e, R2=R: e.activation(
                s_em8, R2["m8"], AF.Exp, bias=s_nm0[:, 0:1]),
                reads=[R["b_m8"], b_nm0], writes=[b_em8])
            plan.op("dve", lambda e: e.reduce_sum(s_wsum, s_em8, AX.X),
                    reads=[b_em8], writes=[b_wsum])
            plan.op("dve", lambda e: e.reciprocal(s_wsum, s_wsum),
                    reads=[b_wsum], writes=[b_wsum], strict=True)
            plan.op("dve", lambda e: e.tensor_scalar(s_w8, s_em8, s_wsum[:, 0:1], None, OP.mult),
                    reads=[b_em8, b_wsum], writes=[b_w8])
            plan.op("dve", lambda e, P2=P: e.tensor_scalar(
                P2["keep"], P2["pos"], 512.0, None, OP.is_lt),
                reads=[P["b_pos"]], writes=[P["b_keep"]])
            plan.op("dve", lambda e, P2=P: e.tensor_tensor(
                s_w8, s_w8, P2["keep"], OP.mult),
                reads=[b_w8, P["b_keep"]], writes=[b_w8], strict=True)
            plan.op("dve", lambda e, P2=P: e.tensor_scalar(
                s_slot, P2["pos"], 511.0, None, OP.min),
                reads=[P["b_pos"]], writes=[b_slot])
            plan.op("dve", lambda e, R2=R: e.scalar_tensor_tensor(
                s_cif, R2["idxf"], 512.0, s_slot, OP.mult, OP.add),
                reads=[R["b_idxf"], b_slot], writes=[b_cif])
            plan.op("dve", lambda e: e.tensor_copy(s_ciu, s_cif), reads=[b_cif], writes=[b_ciu])
            plan.op("sp", lambda e, kk=k_tile: e.dma_start(
                out=cidx_d[kk * 1024:(kk + 1) * 1024, :].rearrange("(p j) x -> p (j x)", p=128),
                in_=s_ciu), reads=[b_ciu], writes=[db(cidx_d, k_tile % 4)], dma=True)
            plan.op("sp", lambda e, kk=k_tile: e.dma_start(
                out=wpr_d[kk * 1024:(kk + 1) * 1024, :].rearrange("(p j) x -> p (j x)", p=128),
                in_=s_w8), reads=[b_w8], writes=[db(wpr_d, k_tile % 4)], dma=True)

    arA.reset(); arB.reset()
    s_xnT, b_xnT = arA.carve_bf16("s_xnT", (128, 8, L))
    s_x2T, b_x2T = arA.carve_bf16("s_x2T", (128, 8, L))
    s_vw, b_vw = arA.carve_bf16("s_vw", (128, 8, D))
    s_nb, b_nb = arA.carve_f32("s_nb", (128, 512))
    s_cm, b_cm = arA.carve_f32("s_cm", (128, 512))
    s_vevac, b_vevac = arA.carve_bf16("s_vevac", (128, 1024))
    s_sdw, b_sdw = arB.carve_bf16("s_sdw", (128, 16, D))
    s_actT, b_actT = arB.carve_bf16("s_actT", (128, 16, 512))
    s_s1, b_s1 = arB.carve_f32("s_s1", (128, 512))
    s_wsgA = [arB.carve_bf16(f"s_wsg1{i}", (128, 8, 128)) for i in range(2)]
    s_wsgB = [arB.carve_bf16(f"s_wsg2{i}", (128, 8, 128)) for i in range(2)]
    s_gbc = [arB.carve_f32("s_gbc0", (128, 512))]
    s_grow, b_grow = arB.carve_f32("s_grow", (128, 512))
    s_shT, b_shT = arB.carve_f32("s_shT", (128, 512))

    plan.op("pool", lambda e: e.dma_start(
        out=s_xnT[:, :, :], in_=xnT_d.rearrange("(c p) t -> p c t", p=128)),
        reads=[], extra_deps=db_all(xnT_d), writes=[b_xnT], dma=True)
    plan.op("pool", lambda e: e.dma_start(
        out=s_x2T[:, :, :], in_=xn2T_d.rearrange("(c p) t -> p c t", p=128)),
        reads=[], extra_deps=db_all(xn2T_d), writes=[b_x2T], dma=True)
    plan.op("sp", lambda e: e.dma_start(
        out=s_vw[:, :, :], in_=vwT_bf.rearrange("(c p) d -> p c d", p=128)),
        reads=[db(vwT_bf)], writes=[b_vw], dma=True)
    plan.op("sp", lambda e: e.dma_start(
        out=s_sdw[:, :, :], in_=sdown_bf.rearrange("(c p) d -> p c d", p=128)),
        reads=[db(sdown_bf)], writes=[b_sdw], dma=True)

    def run_interleaved(*gens):
        gens = list(gens)
        while gens:
            for g2 in list(gens):
                try:
                    next(g2)
                except StopIteration:
                    gens.remove(g2)

    def gen_ph25a():
      for tc in range(4):
        pb0, pbb0 = pd0, pb_d0
        pb1, pbb1 = pd1, pb_d1
        for hc in range(8):
            plan.op("pe", lambda e, h=hc, t=tc: e.matmul(
                pb1[0:64, 0:512], w_xp[:, h, 0:64], s_x2T[:, h, t * 512:(t + 1) * 512],
                start=(h == 0), stop=(h == 7)), reads=[b_wxp, b_x2T], writes=[pbb1])
        for hc in range(8):
            plan.op("pe", lambda e, h=hc, t=tc: e.matmul(
                pb0[0:16, 0:512], w_xp[:, h, 64:80], s_x2T[:, h, t * 512:(t + 1) * 512],
                start=(h == 0), stop=(h == 7)), reads=[b_wxp, b_x2T], writes=[pbb0])
        for hc in range(8):
            plan.op("pe", lambda e, h=hc, t=tc: e.matmul(
                pb0[0:16, 512:1024], w_xp[:, h, 80:96], s_x2T[:, h, t * 512:(t + 1) * 512],
                start=(h == 0), stop=(h == 7)), reads=[b_wxp, b_x2T], writes=[pbb0])
        plan.op("act", lambda e, t=tc: e.copy(s_dtin[:, t * 512:(t + 1) * 512], pb1[0:64, 0:512]),
                reads=[pbb1], writes=[b_dtin])
        plan.op("act", lambda e: e.copy(s_nb[0:16, :], pb0[0:16, 0:512]),
                reads=[pbb0], writes=[b_nb])
        plan.op("act", lambda e: e.copy(s_cm[0:16, :], pb0[0:16, 512:1024]),
                reads=[pbb0], writes=[b_cm])
        plan.op("sp", lambda e, t=tc: e.dma_start(
            out=negB_d[:, t * 512:(t + 1) * 512], in_=s_nb[0:16, :]),
            reads=[b_nb], writes=[db(negB_d, tc % 2)], dma=True)
        plan.op("sp", lambda e, t=tc: e.dma_start(
            out=Cm_d[:, t * 512:(t + 1) * 512], in_=s_cm[0:16, :]),
            reads=[b_cm], writes=[db(Cm_d, tc % 2)], dma=True)
        yield

    dram_in("ones_row_bf", (1, 128), BF16)
    c_onesr, b_onesr = small("c_onesr", (1, 128), BF16)
    ld("sp", c_onesr[:, :], b_onesr, din["ones_row_bf"])

    def gen_ph25b():
      for tb in range(16):
        for dh in range(2):
            pv = pd0[:, dh * 512:(dh + 1) * 512]
            for hc in range(8):
                plan.op("pe", lambda e, h=hc, t=tb, d=dh: e.matmul(
                    pd0[:, d * 512:(d + 1) * 512], s_x2T[:, h, t * 128:(t + 1) * 128],
                    s_vw[:, h, d * 512:(d + 1) * 512],
                    start=(h == 0), stop=False), reads=[b_x2T, b_vw], writes=[pb_d0])
            plan.op("pe", lambda e, t=tb, d=dh: e.matmul(
                pd0[:, d * 512:(d + 1) * 512], c_onesr[:, :], w_vb[0:1, d * 512:(d + 1) * 512],
                start=False, stop=True), reads=[b_onesr, b_wvb], writes=[pb_d0])
            plan.op("act", lambda e, d=dh: e.activation(
                s_vevac[:, d * 512:(d + 1) * 512], pd0[:, d * 512:(d + 1) * 512], AF.Silu),
                reads=[pb_d0], writes=[b_vevac])
        plan.op("sp", lambda e, t=tb: e.dma_start(
            out=v_d[t * 128:(t + 1) * 128, :], in_=s_vevac),
            reads=[b_vevac], writes=[db(v_d, tb % 4)], dma=True)
        yield

    def gen_ph25c():
      for tc in range(4):
        pg, pbg = pk[3]
        for hc in range(8):
            plan.op("pe", lambda e, h=hc, t=tc: e.matmul(
                pg[0:1, 0:512], w_sg[:, h, :], s_xnT[:, h, t * 512:(t + 1) * 512],
                start=(h == 0), stop=(h == 7)), reads=[b_wsg, b_xnT], writes=[pbg])
        plan.op("act", lambda e: e.activation(s_grow[0:1, :], pg[0:1, 0:512], AF.Sigmoid),
                reads=[pbg], writes=[b_grow])
        plan.op("sp", lambda e, t=tc: e.dma_start(
            out=grow_d[0:1, t * 512:(t + 1) * 512], in_=s_grow[0:1, :]),
            reads=[b_grow], writes=[db(grow_d)], dma=True)
        yield

    def gen_ph25d():
      for tc in range(4):
        for fp in range(16):
            wg1, bwg1 = s_wsgA[fp % 2]
            wg2, bwg2 = s_wsgB[fp % 2]
            plan.op("sp", lambda e, f=fp, w=wg1: e.dma_start(
                out=w[:, :, :], in_=sgu_bf[:, f * 128:(f + 1) * 128]
                .rearrange("(c p) f -> p c f", p=128)),
                reads=[db(sgu_bf)], writes=[bwg1], dma=True)
            plan.op("sp", lambda e, f=fp, w=wg2: e.dma_start(
                out=w[:, :, :], in_=sgu_bf[:, ISH + f * 128:ISH + (f + 1) * 128]
                .rearrange("(c p) f -> p c f", p=128)),
                reads=[db(sgu_bf)], writes=[bwg2], dma=True)
            for hc in range(8):
                plan.op("pe", lambda e, h=hc, t=tc, w=wg1: e.matmul(
                    pk[0][0][:, :], w[:, h, :], s_xnT[:, h, t * 512:(t + 1) * 512],
                    start=(h == 0), stop=(h == 7)), reads=[bwg1, b_xnT], writes=[pk[0][1]])
            for hc in range(8):
                plan.op("pe", lambda e, h=hc, t=tc, w=wg2: e.matmul(
                    pk[1][0][:, :], w[:, h, :], s_xnT[:, h, t * 512:(t + 1) * 512],
                    start=(h == 0), stop=(h == 7)), reads=[bwg2, b_xnT], writes=[pk[1][1]])
            plan.op("act", lambda e: e.activation(s_s1, pk[0][0][:, :], AF.Silu),
                    reads=[pk[0][1]], writes=[b_s1])
            plan.op("dve", lambda e, f=fp: e.tensor_tensor(
                s_actT[:, f, :], s_s1, pk[1][0][:, :], OP.mult),
                reads=[b_s1, pk[1][1]], writes=[b_actT])
            yield
        plan.op("sp", lambda e, t=tc: e.dma_start(
            out=s_gbc[0][0],
            in_=grow_d[0:1, t * 512:(t + 1) * 512].partition_broadcast(128)),
            reads=[db(grow_d)], writes=[s_gbc[0][1]], dma=True)
        for dblk in range(8):
            pj, pbj = pk[2] if dblk % 2 == 0 else pk[3]
            for ic in range(16):
                plan.op("pe", lambda e, i=ic, d=dblk, p=pj: e.matmul(
                    p[:, :], s_sdw[:, i, d * 128:(d + 1) * 128], s_actT[:, i, :],
                    start=(i == 0), stop=(i == 15)), reads=[b_sdw, b_actT], writes=[pbj])
            plan.op("dve", lambda e, d=dblk, p=pj: e.tensor_tensor(
                s_shT, p[:, :], s_gbc[0][0], OP.mult),
                reads=[pbj, s_gbc[0][1]], writes=[b_shT])
            plan.op("sp", lambda e, d=dblk, t=tc: e.dma_start(
                out=sharedT_d[d * 128:(d + 1) * 128, t * 512:(t + 1) * 512], in_=s_shT),
                reads=[b_shT], writes=[db(sharedT_d, (tc * 8 + dblk) % 8)], dma=True)
            yield

    def gen_ph25():
        for g2 in (gen_ph25a(), gen_ph25b(), gen_ph25c(), gen_ph25d()):
            for _ in g2:
                yield

    run_interleaved(gen_passB(), gen_ph25())

    L_d = dram_t("L_d", (D, L))
    
    arB.reset()
    s_sg1, b_sg1 = arB.carve_f32("s_sg1", (128, 512))
    s_sg2, b_sg2 = arB.carve_f32("s_sg2", (128, 512))
    s_sg3, b_sg3 = arB.carve_f32("s_sg3", (128, 512))
    for tc in range(4):
        for g in range(8):
            pq, pbq = pk[g % 2]
            plan.op("pe", lambda e, gg=g, t=tc, p=pq: e.matmul(
                p[:, :], w_dt[:, gg, :], s_dtin[:, t * 512:(t + 1) * 512],
                start=True, stop=True), reads=[b_wdt, b_dtin], writes=[pbq])
            plan.op("act", lambda e, gg=g, p=pq: e.activation(
                s_sg1, p[:, :], AF.Sigmoid, scale=-1.0, bias=s_ndtb[:, gg:gg + 1]),
                reads=[pbq, b_ndtb], writes=[b_sg1])
            plan.op("dve", lambda e: e.tensor_scalar(s_sg2, s_sg1, 1e-38, None, OP.max),
                    reads=[b_sg1], writes=[b_sg2])
            plan.op("act", lambda e: e.activation(s_sg3, s_sg2, AF.Ln),
                    reads=[b_sg2], writes=[b_sg3])
            if tc == 0 and g == 0:
                dump_now("sg1d", s_sg1, b_sg1, [128, 512], F32)
                s_dtr, b_dtr = small("s_dtr", (128, 512))
                plan.op("act", lambda e, p=pq: e.copy(s_dtr[:, :], p[:, :]),
                        reads=[pbq], writes=[b_dtr])
                dump_now("dtrawd", s_dtr[:, :], b_dtr, [128, 512], F32)
                dump_now("wdtd", w_dt[:, :, :], b_wdt, [64, 8, 128], BF16)
                dump_now("ndtbd", s_ndtb[:, :], b_ndtb, [128, 8], F32)
            plan.op("sp", lambda e, gg=g, t=tc: e.dma_start(
                out=L_d[gg * 128:(gg + 1) * 128, t * 512:(t + 1) * 512], in_=s_sg3),
                reads=[b_sg3], writes=[db(L_d, (tc * 8 + g) % 8)], dma=True)

    arA.reset()
    s_uc = [arA.carve_f32("s_uc0", (128, 8, 512))]
    s_Lc = [arA.carve_f32("s_Lc0", (128, 8, 512))]
    s_du, b_du = arA.carve_f32("s_du", (128, 8, 512))
    s_ya, b_ya = arA.carve_f32("s_ya", (128, 8, 512))
    s_bcB = [arA.carve_f32(f"s_bcB{i}", (128, 512)) for i in range(2)]
    s_bcC = [arA.carve_f32(f"s_bcC{i}", (128, 512)) for i in range(2)]
    s_dA = [arA.carve_f32(f"s_dA{i}", (128, 512)) for i in range(4)]
    s_xB = [arA.carve_f32(f"s_xB{i}", (128, 512)) for i in range(2)]
    s_hh = [arA.carve_f32(f"s_hh{i}", (128, 512)) for i in range(2)]
    _thy, b_hy = small("s_hy_t", (128, 512))
    s_hy = _thy[:, :]
    s_cry = []
    for _i in range(2):
        _tc2, _bc2 = small(f"s_cry_t{_i}", (128, 128))
        s_cry.append((_tc2[:, :], _bc2))
    s_mxc, b_mxc = arA.carve_bf16("s_mxc", (128, 8, 512))

    def gen_scan():
      for c in range(4):
        su, bu = s_uc[0]
        sL, bL = s_Lc[0]
        plan.op("sp", lambda e, cc=c, d=su: e.dma_start(
            out=d[:, :, :], in_=xn2T_d.rearrange("(g p) t -> p g t", p=128)[:, :, cc * 512:(cc + 1) * 512]),
            reads=[], extra_deps=db_all(xn2T_d), writes=[bu], dma=True)
        plan.op("sp", lambda e, cc=c, d=sL: e.dma_start(
            out=d[:, :, :], in_=L_d.rearrange("(g p) t -> p g t", p=128)[:, :, cc * 512:(cc + 1) * 512]),
            reads=[], extra_deps=db_all(L_d), writes=[bL], dma=True)
        for g in range(8):
            plan.op("dve", lambda e, gg=g: e.scalar_tensor_tensor(
                s_du[:, gg, :], sL[:, gg, :], -1.0, su[:, gg, :], OP.mult, OP.mult),
                reads=[bL, bu], writes=[b_du])
            plan.op("dve", lambda e, gg=g: e.tensor_scalar(
                s_ya[:, gg, :], su[:, gg, :], cols["D_p"][0][:, gg:gg + 1], None, OP.mult),
                reads=[bu, cols["D_p"][1]], writes=[b_ya])
        for n in range(NST):
            sbB, bbB = s_bcB[n % 2]
            sbC, bbC = s_bcC[n % 2]
            plan.op("sp", lambda e, nn=n, cc=c, d=sbB: e.dma_start(
                out=d, in_=negB_d[nn:nn + 1, cc * 512:(cc + 1) * 512].partition_broadcast(128)),
                reads=[], extra_deps=db_all(negB_d), writes=[bbB], dma=True)
            plan.op("sp", lambda e, nn=n, cc=c, d=sbC: e.dma_start(
                out=d, in_=Cm_d[nn:nn + 1, cc * 512:(cc + 1) * 512].partition_broadcast(128)),
                reads=[], extra_deps=db_all(Cm_d), writes=[bbC], dma=True)
            for g in range(8):
                it = (n * 8 + g) % 2
                sdA, bdA = s_dA[(n * 8 + g) % 4]
                sxB, bxB = s_xB[it]
                shh, bhh = s_hh[it]
                col = g * NST + n
                plan.op("act", lambda e, gg=g, cl=col, d=sdA: e.activation(
                    d, sL[:, gg, :], AF.Exp, scale=s_expA[:, cl:cl + 1]),
                    reads=[bL, b_expA], writes=[bdA])
                plan.op("dve", lambda e, gg=g, d=sxB, sB=sbB: e.tensor_tensor(
                    d, s_du[:, gg, :], sB, OP.mult),
                    reads=[b_du, bbB], writes=[bxB])
                if c == 0:
                    plan.op("dve", lambda e, d=shh, a=sdA, x=sxB: e.tensor_tensor_scan(
                        d, a, x, 0.0, OP.mult, OP.add),
                        reads=[bdA, bxB], writes=[bhh])
                else:
                    cin, bcin = s_cry[(c + 1) % 2]
                    plan.op("dve", lambda e, d=shh, a=sdA, x=sxB, ci=cin, cl=col:
                            e.tensor_tensor_scan(d, a, x, ci[:, cl:cl + 1], OP.mult, OP.add),
                            reads=[bdA, bxB, bcin], writes=[bhh])
                cout, bcout = s_cry[c % 2]
                plan.op("dve", lambda e, d=shh, co=cout, cl=col: e.tensor_copy(
                    co[:, cl:cl + 1], d[:, 511:512]),
                    reads=[bhh], writes=[bcout])
                plan.op("dve", lambda e, d=shh, sC=sbC: e.tensor_tensor(
                    s_hy, d, sC, OP.mult), reads=[bhh, bbC], writes=[b_hy])
                plan.op("dve", lambda e, gg=g: e.tensor_tensor(
                    s_ya[:, gg, :], s_ya[:, gg, :], s_hy, OP.add),
                    reads=[b_hy, b_ya], writes=[b_ya], strict=True)
            yield
        for g in range(8):
            plan.op("act", lambda e, gg=g: e.activation(
                s_mxc[:, gg, :], s_ya[:, gg, :], AF.Silu),
                reads=[b_ya], writes=[b_mxc])
        plan.op("sp", lambda e, cc=c: e.dma_start(
            out=mxT_d.rearrange("(g p) t -> p g t", p=128)[:, :, cc * 512:(cc + 1) * 512],
            in_=s_mxc[:, :, :]), reads=[b_mxc], writes=[db(mxT_d, c)], dma=True)

    arB.reset()
    s_wg = [arB.carve_bf16(f"s_wg{i}", (128, 8, 1024)) for i in range(2)]
    s_wd, b_wd = arB.carve_bf16("s_wd", (128, 4, 1024))
    s_gtrawt, b_gtraw = small("s_gtrawt", (128, 128), U32)
    s_gtraw = s_gtrawt[:, :]
    s_gtTt, b_gtT = small("s_gtTt", (128, 256), U32)
    s_gtT = s_gtTt[:, :]
    s_braw = [arB.carve_bf16(f"s_braw{i}", (128, 1024)) for i in range(4)]
    s_bufT, b_bufT = arB.carve_bf16("s_bufT", (128, 8, 512))
    s_es1, b_es1 = arB.carve_f32("s_es1", (128, 512))
    s_eact, b_eact = arB.carve_bf16("s_eact", (128, 4, 512))
    s_eout, b_eout = arB.carve_bf16("s_eout", (128, 1024))

    for half in range(2):
        plan.op("sp", lambda e, hh=half: e.dma_start(
            out=s_gtraw,
            in_=gtab_d[hh * 16384:(hh + 1) * 16384, :].rearrange("(g i) x -> g (i x)", g=128)),
            reads=[], extra_deps=db_all(gtab_d), writes=[b_gtraw], dma=True)
        pt, pbt = pk[0]
        plan.op("pe", lambda e, p=pt: e.transpose(
            p[:, 0:128], s_gtraw.bitcast(F32), c_ident[:, :]),
            reads=[b_gtraw, b_ident], writes=[pbt])
        plan.op("dve", lambda e, hh=half, p=pt: e.tensor_copy(
            s_gtT.bitcast(F32)[:, hh * 128:(hh + 1) * 128], p[:, 0:128]),
            reads=[pbt], writes=[b_gtT], strict=True)

    def gen_experts():
      for ex in range(E):
        wg, bwg = s_wg[ex % 2]
        plan.op("pool", lambda e, x=ex, w=wg: e.dma_start(
            out=w[:, :, :], in_=din["egu_w"][x, :, :].rearrange("(hc p) f -> p hc f", p=128)),
            writes=[bwg], dma=True)
        for sbk in range(4):
            sbr, bbr = s_braw[sbk]
            gcol = ex * 4 + sbk
            plan.op("pool", lambda e, gc=gcol, d=sbr: e.indirect_dma_start(
                d, None, xn_all, IndirectOffsetOnAxis(s_gtT[:, gc:gc + 1], 0),
                bounds_check=bcreg(e, TA - 1), oob_is_err=False),
                reads=[b_gtT], extra_deps=db_all(xn_all), writes=[bbr], dma=True)
        for hc in range(8):
            ptb, pbtb = pk[2] if hc % 2 == 0 else pk[3]
            for sbk in range(4):
                sbr, bbr = s_braw[sbk]
                plan.op("pe", lambda e, h=hc, t=sbk, d=sbr, p=ptb: e.transpose(
                    p[:, t * 64:(t + 1) * 64].bitcast(BF16),
                    d[:, h * 128:(h + 1) * 128], c_identb[:, :]),
                    reads=[bbr, b_identb], writes=[pbtb])
            plan.op("act", lambda e, h=hc, p=ptb: e.copy(
                s_bufT[:, h, :], p[:, 0:256].bitcast(BF16)),
                reads=[pbtb], writes=[b_bufT], strict=True)
        plan.op("pool", lambda e, x=ex: e.dma_start(
            out=s_wd[:, :, :], in_=din["edown_w"][x, :, :].rearrange("(ic p) d -> p ic d", p=128)),
            writes=[b_wd], dma=True)
        for ic in range(4):
            pg_g, pbg_g = pk[0]
            pg_u, pbg_u = pk[1]
            for hc in range(8):
                plan.op("pe", lambda e, h=hc, i=ic, w=wg: e.matmul(
                    pg_g[:, :], w[:, h, i * 128:(i + 1) * 128], s_bufT[:, h, :],
                    start=(h == 0), stop=(h == 7)), reads=[bwg, b_bufT], writes=[pbg_g])
            for hc in range(8):
                plan.op("pe", lambda e, h=hc, i=ic, w=wg: e.matmul(
                    pg_u[:, :], w[:, h, 512 + i * 128:512 + (i + 1) * 128], s_bufT[:, h, :],
                    start=(h == 0), stop=(h == 7)), reads=[bwg, b_bufT], writes=[pbg_u])
            plan.op("act", lambda e: e.activation(s_es1, pg_g[:, :], AF.Silu),
                    reads=[pbg_g], writes=[b_es1])
            plan.op("dve", lambda e, i=ic: e.tensor_tensor(
                s_eact[:, i, :], s_es1, pg_u[:, :], OP.mult),
                reads=[b_es1, pbg_u], writes=[b_eact])
        for tb in range(4):
            for ic in range(4):
                plan.op("pe", lambda e, i=ic, t=tb: e.matmul(
                    pd0[:, 0:512], s_eact[:, i, t * 128:(t + 1) * 128], s_wd[:, i, 0:512],
                    start=(i == 0), stop=(i == 3)), reads=[b_eact, b_wd], writes=[pb_d0])
            for ic in range(4):
                plan.op("pe", lambda e, i=ic, t=tb: e.matmul(
                    pd0[:, 512:1024], s_eact[:, i, t * 128:(t + 1) * 128], s_wd[:, i, 512:1024],
                    start=(i == 0), stop=(i == 3)), reads=[b_eact, b_wd], writes=[pb_d0])
            plan.op("act", lambda e: e.copy(s_eout[:, 0:512], pd0[:, 0:512]),
                    reads=[pb_d0], writes=[b_eout])
            plan.op("act", lambda e: e.copy(s_eout[:, 512:1024], pd0[:, 512:1024]),
                    reads=[pb_d0], writes=[b_eout], strict=True)
            plan.op("sp", lambda e, x=ex, t=tb: e.dma_start(
                out=out_all[x * 512 + t * 128:x * 512 + (t + 1) * 128, :], in_=s_eout),
                reads=[b_eout], writes=[db(out_all, ex % 8)], dma=True)
        yield

    run_interleaved(gen_scan(), gen_experts())

    arA.reset()
    s_ciTt, b_ciT = small("s_ciTt", (128, 128), U32)
    s_ciT = s_ciTt[:, :]
    s_wpT, b_wpT = arA.carve_f32("s_wpT", (128, 128))
    s_tmpT, b_tmpT = arA.carve_f32("s_tmpT", (128, 128))
    s_crow = [arA.carve_bf16(f"s_crow{i}", (128, 1024)) for i in range(2)]
    s_wblkf, b_wblkf = arA.carve_f32("s_wblkf", (128, 16))
    s_wblk, b_wblk = arA.carve_bf16("s_wblk", (128, 16))
    s_ym, b_ym = arA.carve_f32("s_ym", (128, 1024))

    for half in range(2):
        plan.op("sp", lambda e, hh=half: e.dma_start(
            out=s_tmpT.bitcast(U32)[0:64, 0:128],
            in_=cidx_d[hh * 8192:(hh + 1) * 8192, :].rearrange("(g i) x -> g (i x)", g=64)),
            reads=[], extra_deps=db_all(cidx_d), writes=[b_tmpT], dma=True)
        pt, pbt = pk[0]
        plan.op("pe", lambda e, p=pt: e.transpose(p[:, 0:64], s_tmpT[0:64, 0:128], c_ident[0:64, 0:64]),
                reads=[b_tmpT, b_ident], writes=[pbt])
        plan.op("dve", lambda e, hh=half, p=pt: e.tensor_copy(
            s_ciT.bitcast(F32)[:, hh * 64:(hh + 1) * 64], p[:, 0:64]),
            reads=[pbt], writes=[b_ciT], strict=True)
        plan.op("sp", lambda e, hh=half: e.dma_start(
            out=s_tmpT[0:64, 0:128],
            in_=wpr_d[hh * 8192:(hh + 1) * 8192, :].rearrange("(g i) x -> g (i x)", g=64)),
            reads=[], extra_deps=db_all(wpr_d), writes=[b_tmpT], dma=True)
        plan.op("pe", lambda e, p=pt: e.transpose(p[:, 0:64], s_tmpT[0:64, 0:128], c_ident[0:64, 0:64]),
                reads=[b_tmpT, b_ident], writes=[pbt], strict=True)
        plan.op("dve", lambda e, hh=half, p=pt: e.tensor_copy(
            s_wpT[:, hh * 64:(hh + 1) * 64], p[:, 0:64]),
            reads=[pbt], writes=[b_wpT], strict=True)

    def gen_ph4a():
      for g in range(128):
        scr, bcr = s_crow[g % 2]
        plan.op("pool", lambda e, gg=g, d=scr: e.indirect_dma_start(
            d, None, out_all, IndirectOffsetOnAxis(s_ciT[:, gg:gg + 1], 0),
            bounds_check=bcreg(e, E * CAP - 1), oob_is_err=False),
            reads=[b_ciT], extra_deps=db_all(out_all), writes=[bcr], dma=True)
        plan.op("dve", lambda e, gg=g: e.tensor_scalar(
            s_wblkf, c_blk[:, :], s_wpT[:, gg:gg + 1], None, OP.mult),
            reads=[b_blk, b_wpT], writes=[b_wblkf])
        plan.op("dve", lambda e: e.tensor_copy(s_wblk, s_wblkf),
                reads=[b_wblkf], writes=[b_wblk], strict=True)
        pc, pbc = pk[1] if g % 2 == 0 else pk[2]
        pc2, pbc2 = pk[3]
        plan.op("pe", lambda e, d=scr, p=pc: e.matmul(
            p[0:16, 0:512], s_wblk[:, :], d[:, 0:512], start=True, stop=True),
            reads=[b_wblk, bcr], writes=[pbc])
        plan.op("pe", lambda e, d=scr, p=pc2: e.matmul(
            p[0:16, 0:512], s_wblk[:, :], d[:, 512:1024], start=True, stop=True),
            reads=[b_wblk, bcr], writes=[pbc2])
        plan.op("dve", lambda e, p=pc: e.tensor_copy(s_ym[0:16, 0:512], p[0:16, 0:512]),
                reads=[pbc], writes=[b_ym])
        plan.op("dve", lambda e, p=pc2: e.tensor_copy(s_ym[0:16, 512:1024], p[0:16, 0:512]),
                reads=[pbc2], writes=[b_ym], strict=True)
        plan.op("sp", lambda e, gg=g: e.dma_start(
            out=ymoe_d[gg * 16:(gg + 1) * 16, :], in_=s_ym[0:16, :]),
            reads=[b_ym], writes=[db(ymoe_d, g % 4)], dma=True)
        yield

    s_mxT, b_mxT = arA.carve_bf16("s_mxT", (128, 8, 512))
    s_mxw = [arA.carve_bf16(f"s_mxw{i}", (128, 8, 128)) for i in range(2)]
    s_qT, b_qT = arA.carve_bf16("s_qT", (128, L))
    s_kT, b_kT = arA.carve_bf16("s_kT", (128, L))
    s_aT, b_aT = arA.carve_bf16("s_aT", (128, L))
    s_bT, b_bT = arA.carve_bf16("s_bT", (128, L))
    s_bstage, b_bstage = arA.carve_f32("s_bstage", (128, 512))
    s_z1, b_z1 = arA.carve_f32("s_z1", (128, 512))
    s_sc, b_scb = arA.carve_f32("s_sc", (128, L))
    s_p32, b_p32 = arA.carve_f32("s_p32", (128, L))
    s_pbf, b_pbf = arA.carve_bf16("s_pbf", (128, L))
    s_atT, b_atT = arA.carve_bf16("s_atT", (128, 16, 128))
    s_vv = [arA.carve_bf16(f"s_vv{i}", (128, 1024)) for i in range(6)]
    s_hw, b_hw = arA.carve_bf16("s_hw", (128, 8, 1024))
    plan.op("sp", lambda e: e.dma_start(
        out=s_hw[:, :, :], in_=hwT_bf.rearrange("(c p) d -> p c d", p=128)),
        reads=[db(hwT_bf)], writes=[b_hw], dma=True)
    s_hro, b_hro = arA.carve_bf16("s_hro", (128, 8, 128))
    s_smcol, b_smcol = arA.carve_f32("s_smcol", (128, 4))
    arB.reset()
    s_csn, b_csn = arB.carve_f32("s_csn", (128, 512))
    s_sn2, b_sn2 = arB.carve_f32("s_sn2", (128, 512))
    s_t1b, b_t1b = arB.carve_f32("s_t1b", (128, 512))
    s_rre, b_rre = arB.carve_f32("s_rre", (128, 8, 128))
    s_hxe, b_hxe = arB.carve_f32("s_hxe", (128, 8, 128))
    s_ue, b_ue = arB.carve_f32("s_ue", (128, 8, 128))
    s_xne, b_xne = arB.carve_f32("s_xne", (128, 8, 128))
    s_she, b_she = arB.carve_f32("s_she", (128, 8, 128))
    s_e1, b_e1 = arB.carve_f32("s_e1", (128, 128))
    s_hf, b_hf = arB.carve_f32("s_hf", (128, 128))
    s_sumT, b_sumT = arB.carve_f32("s_sumT", (128, 8, 128))
    s_xo, b_xo = arB.carve_f32("s_xo", (128, 1024))
    s_ymo, b_ymo = arB.carve_f32("s_ymo", (128, 1024))
    s_outt, b_outt = arB.carve_f32("s_outt", (128, 1024))
    s_vv += [arB.carve_bf16(f"s_vv{i + 6}", (128, 1024)) for i in range(10)]
    for vb in range(16):
        vt, bvt = s_vv[vb]
        plan.op("sp", lambda e, s2=vb, d=vt: e.dma_start(
            out=d, in_=v_d[s2 * 128:(s2 + 1) * 128, :]),
            reads=[], extra_deps=db_all(v_d), writes=[bvt], dma=True)

    for cc in range(4):
        plan.op("sp", lambda e, c2=cc: e.dma_start(
            out=s_csn, in_=din["cos2"][:, c2 * 512:(c2 + 1) * 512]),
            writes=[b_csn], dma=True)
        plan.op("sp", lambda e, c2=cc: e.dma_start(
            out=s_sn2, in_=din["sin2"][:, c2 * 512:(c2 + 1) * 512]),
            writes=[b_sn2], dma=True)
        for (dst, bdst, c1, c2n) in [(s_aT, b_aT, "A1_col", "A2_col"),
                                     (s_bT, b_bT, "B1_col", "B2_col")]:
            plan.op("dve", lambda e, cn=c1: e.tensor_scalar(
                s_t1b, s_csn, cols[cn][0][:, 0:1], None, OP.mult),
                reads=[b_csn, cols[c1][1]], writes=[b_t1b])
            plan.op("dve", lambda e, cn=c2n, d=dst, c3=cc: e.scalar_tensor_tensor(
                d[:, c3 * 512:(c3 + 1) * 512], s_sn2, cols[cn][0][:, 0:1], s_t1b,
                OP.mult, OP.add),
                reads=[b_sn2, cols[c2n][1], b_t1b], writes=[bdst])

    dump_now("aTd", s_aT, b_aT, [128, L], BF16)
    dump_now("bTd", s_bT, b_bT, [128, L], BF16)
    def gen_ph4b():
      for tc in range(4):
        plan.op("sp", lambda e, c=tc: e.dma_start(
            out=s_mxT[:, :, :],
            in_=mxT_d.rearrange("(g p) t -> p g t", p=128)[:, :, c * 512:(c + 1) * 512]),
            reads=[db(mxT_d, tc)], writes=[b_mxT], dma=True)
        for fb in range(25):
            wmx, bwmx = s_mxw[fb % 2]
            plan.op("sp", lambda e, f=fb, w=wmx: e.dma_start(
                out=w[:, :, :], in_=mxwT_bf[:, f * 128:(f + 1) * 128]
                .rearrange("(c p) f -> p c f", p=128)),
                reads=[db(mxwT_bf)], writes=[bwmx], dma=True)
            pview = pd1[:, (fb % 2) * 512:(fb % 2) * 512 + 512]
            for hc in range(8):
                plan.op("pe", lambda e, h=hc, w=wmx, pv=pview: e.matmul(
                    pv, w[:, h, :], s_mxT[:, h, :],
                    start=(h == 0), stop=(h == 7)), reads=[bwmx, b_mxT], writes=[pb_d1])
            mb = cols["mx_b"]
            if fb < 8:
                plan.op("act", lambda e, f=fb, pv=pview: e.activation(
                    s_bstage, pv, AF.Sigmoid, bias=mb[0][:, f:f + 1]),
                    reads=[pb_d1, mb[1]], writes=[b_bstage])
                plan.op("sp", lambda e, f=fb, c=tc: e.dma_start(
                    out=uT_d[f * 128:(f + 1) * 128, c * 512:(c + 1) * 512], in_=s_bstage),
                    reads=[b_bstage], writes=[db(uT_d, (fb + tc) % 4)], dma=True)
            elif fb == 8:
                plan.op("act", lambda e, f=fb, pv=pview: e.activation(
                    s_z1, pv, AF.Silu, bias=mb[0][:, f:f + 1]),
                    reads=[pb_d1, mb[1]], writes=[b_z1])
                plan.op("dve", lambda e, c=tc: e.tensor_scalar(
                    s_qT[:, c * 512:(c + 1) * 512], s_z1, s_qg[:, 0:1], s_qb[:, 0:1],
                    OP.mult, OP.add), reads=[b_z1, b_qg, b_qb], writes=[b_qT])
                plan.op("dve", lambda e, c=tc: e.tensor_scalar(
                    s_kT[:, c * 512:(c + 1) * 512], s_z1,
                    cols["g1_col"][0][:, 0:1], cols["b1_col"][0][:, 0:1],
                    OP.mult, OP.add),
                    reads=[b_z1, cols["g1_col"][1], cols["b1_col"][1]], writes=[b_kT])
            elif fb < 17:
                plan.op("act", lambda e, f=fb, pv=pview: e.activation(
                    s_bstage, pv, AF.Silu, bias=mb[0][:, f:f + 1]),
                    reads=[pb_d1, mb[1]], writes=[b_bstage])
                plan.op("sp", lambda e, f=fb, c=tc: e.dma_start(
                    out=rT_d[(f - 9) * 128:(f - 8) * 128, c * 512:(c + 1) * 512],
                    in_=s_bstage), reads=[b_bstage], writes=[db(rT_d, (fb + tc) % 4)], dma=True)
            else:
                plan.op("act", lambda e, f=fb, pv=pview: e.activation(
                    s_bstage, pv, AF.Identity, bias=mb[0][:, f:f + 1]),
                    reads=[pb_d1, mb[1]], writes=[b_bstage])
                plan.op("sp", lambda e, f=fb, c=tc: e.dma_start(
                    out=hxT_d[(f - 17) * 128:(f - 16) * 128, c * 512:(c + 1) * 512],
                    in_=s_bstage), reads=[b_bstage], writes=[db(hxT_d, (fb + tc) % 4)], dma=True)
            yield tc * 32 + fb

        if tc == 0:
            dump_now("qTd", s_qT[:, 0:512], b_qT, [128, 512], BF16)
            dump_now("kTd", s_kT[:, 0:512], b_kT, [128, 512], BF16)
        for i in range(4):
            ib = tc * 4 + i
            yield 8 * (ib + 1)
            send = (ib + 1) * 128
            nsc = (send + 511) // 512
            for sc in range(nsc):
                w = min(512, send - sc * 512)
                pq, pbq = pk[sc % 2]
                plan.op("pe", lambda e, b2=ib, s2=sc, wd=w, p=pq: e.matmul(
                    p[:, 0:wd], s_qT[:, b2 * 128:(b2 + 1) * 128],
                    s_kT[:, s2 * 512:s2 * 512 + wd], start=True, stop=False),
                    reads=[b_qT, b_kT], writes=[pbq])
                plan.op("pe", lambda e, b2=ib, s2=sc, wd=w, p=pq: e.matmul(
                    p[:, 0:wd], s_aT[:, b2 * 128:(b2 + 1) * 128],
                    s_bT[:, s2 * 512:s2 * 512 + wd], start=False, stop=True),
                    reads=[b_aT, b_bT], writes=[pbq])
                if sc == nsc - 1:
                    if w > 128:
                        plan.op("dve", lambda e, s2=sc, wd=w, p=pq: e.tensor_copy(
                            s_sc[:, s2 * 512:s2 * 512 + wd - 128], p[:, 0:wd - 128]),
                            reads=[pbq], writes=[b_scb])
                    plan.op("dve", lambda e, s2=sc, wd=w, p=pq: e.tensor_tensor(
                        s_sc[:, s2 * 512 + wd - 128:s2 * 512 + wd],
                        p[:, wd - 128:wd], c_diag[:, :], OP.add),
                        reads=[pbq, b_diag], writes=[b_scb])
                else:
                    plan.op("dve", lambda e, s2=sc, wd=w, p=pq: e.tensor_copy(
                        s_sc[:, s2 * 512:s2 * 512 + wd], p[:, 0:wd]),
                        reads=[pbq], writes=[b_scb])
            if ib == 0:
                dump_now("scd", s_sc[:, 0:128], b_scb, [128, 128], F32)
            plan.op("dve", lambda e, se=send: e.reduce_max(
                s_smcol[:, 0:1], s_sc[:, 0:se], AX.X), reads=[b_scb], writes=[b_smcol])
            plan.op("dve", lambda e: e.tensor_scalar(
                s_smcol[:, 1:2], s_smcol[:, 0:1], -1.0, None, OP.mult),
                reads=[b_smcol], writes=[b_smcol], strict=True)
            plan.op("act", lambda e, se=send: e.activation(
                s_p32[:, 0:se], s_sc[:, 0:se], AF.Exp, bias=s_smcol[:, 1:2],
                accum_out=s_smcol[:, 2:3]),
                reads=[b_scb, b_smcol], writes=[b_p32, b_smcol])
            plan.op("dve", lambda e: e.reciprocal(s_smcol[:, 3:4], s_smcol[:, 2:3]),
                    reads=[b_smcol], writes=[b_smcol], strict=True)
            plan.op("act", lambda e, se=send: e.activation(
                s_pbf[:, 0:se], s_p32[:, 0:se], AF.Identity, scale=s_smcol[:, 3:4]),
                reads=[b_p32, b_smcol], writes=[b_pbf])
            if ib == 0:
                dump_now("pbfd", s_pbf[:, 0:128], b_pbf, [128, 128], BF16)
                dump_now("smcd", s_smcol, b_smcol, [128, 4], F32)
            for sb in range(ib + 1):
                plan.op("pe", lambda e, s2=sb: e.transpose(
                    pk[2][0][:, 0:64].bitcast(BF16),
                    s_pbf[:, s2 * 128:(s2 + 1) * 128], c_identb[:, :]),
                    reads=[b_pbf, b_identb], writes=[pk[2][1]])
                plan.op("dve", lambda e, s2=sb: e.tensor_copy(
                    s_atT[:, s2, :], pk[2][0][:, 0:64].bitcast(BF16)),
                    reads=[pk[2][1]], writes=[b_atT], strict=True)
            for d8 in range(8):
                for sb in range(ib + 1):
                    vt, bvt = s_vv[sb]
                    plan.op("pe", lambda e, s2=sb, dd=d8, v2=vt, b2=ib: e.matmul(
                        pd0[:, dd * 128:(dd + 1) * 128],
                        v2[:, dd * 128:(dd + 1) * 128], s_atT[:, s2, :],
                        start=(s2 == 0), stop=(s2 == b2)),
                        reads=[bvt, b_atT], writes=[pb_d0])
            if ib == 0 and "pavd" in debug_outs:
                s_pav, b_pav = small("s_pav", (128, 1024))
                for _d8 in range(2):
                    plan.op("act", lambda e, d=_d8: e.copy(
                        s_pav[:, d * 512:(d + 1) * 512], pd0[:, d * 512:(d + 1) * 512]),
                        reads=[pb_d0], writes=[b_pav])
                dump_now("pavd", s_pav[:, :], b_pav, [128, 1024], F32)
            plan.op("sp", lambda e, b2=ib: e.dma_start(
                out=s_rre[:, :, :],
                in_=rT_d.rearrange("(g p) t -> p g t", p=128)[:, :, b2 * 128:(b2 + 1) * 128]),
                reads=[], extra_deps=db_all(rT_d), writes=[b_rre], dma=True)
            for g8 in range(8):
                plan.op("dve", lambda e, g=g8: e.tensor_tensor(
                    s_hro[:, g, :], pd0[:, g * 128:(g + 1) * 128], s_rre[:, g, :], OP.mult),
                    reads=[pb_d0, b_rre], writes=[b_hro])
            plan.op("sp", lambda e, b2=ib: e.dma_start(
                out=s_hxe[:, :, :],
                in_=hxT_d.rearrange("(g p) t -> p g t", p=128)[:, :, b2 * 128:(b2 + 1) * 128]),
                reads=[], extra_deps=db_all(hxT_d), writes=[b_hxe], dma=True)
            plan.op("sp", lambda e, b2=ib: e.dma_start(
                out=s_ue[:, :, :],
                in_=uT_d.rearrange("(g p) t -> p g t", p=128)[:, :, b2 * 128:(b2 + 1) * 128]),
                reads=[], extra_deps=db_all(uT_d), writes=[b_ue], dma=True)
            plan.op("sp", lambda e, b2=ib: e.dma_start(
                out=s_xne[:, :, :],
                in_=xnT_d.rearrange("(g p) t -> p g t", p=128)[:, :, b2 * 128:(b2 + 1) * 128]),
                reads=[], extra_deps=db_all(xnT_d), writes=[b_xne], dma=True)
            plan.op("sp", lambda e, b2=ib: e.dma_start(
                out=s_she[:, :, :],
                in_=sharedT_d.rearrange("(g p) t -> p g t", p=128)[:, :, b2 * 128:(b2 + 1) * 128]),
                reads=[], extra_deps=db_all(sharedT_d), writes=[b_she], dma=True)
            plan.op("sp", lambda e, b2=ib: e.dma_start(
                out=s_xo, in_=x_own[b2 * 128:(b2 + 1) * 128, :]), writes=[b_xo], dma=True)
            plan.op("sp", lambda e, b2=ib: e.dma_start(
                out=s_ymo, in_=ymoe_d[b2 * 128:(b2 + 1) * 128, :]),
                reads=[], extra_deps=db_all(ymoe_d), writes=[b_ymo], dma=True)
            for j8 in range(8):
                pj, pbj = pk[3]
                for dc in range(8):
                    plan.op("pe", lambda e, d2=dc, j=j8, p=pj: e.matmul(
                        p[:, 0:128], s_hw[:, d2, j * 128:(j + 1) * 128], s_hro[:, d2, :],
                        start=(d2 == 0), stop=(d2 == 7)),
                        reads=[b_hw, b_hro], writes=[pbj])
                plan.op("dve", lambda e, j=j8, p=pj: e.tensor_tensor(
                    s_e1, p[:, 0:128], s_hxe[:, j, :], OP.add),
                    reads=[pbj, b_hxe], writes=[b_e1])
                plan.op("act", lambda e, j=j8: e.activation(
                    s_hf, s_e1, AF.Silu, bias=cols["h_b"][0][:, j:j + 1]),
                    reads=[b_e1, cols["h_b"][1]], writes=[b_hf])
                plan.op("dve", lambda e, j=j8: e.tensor_tensor(
                    s_e1, s_hf, s_xne[:, j, :], OP.subtract),
                    reads=[b_hf, b_xne], writes=[b_e1], strict=True)
                plan.op("dve", lambda e, j=j8: e.tensor_tensor(
                    s_e1, s_e1, s_ue[:, j, :], OP.mult),
                    reads=[b_e1, b_ue], writes=[b_e1], strict=True)
                plan.op("dve", lambda e, j=j8: e.tensor_tensor(
                    s_e1, s_e1, s_xne[:, j, :], OP.add),
                    reads=[b_e1, b_xne], writes=[b_e1], strict=True)
                plan.op("dve", lambda e, j=j8: e.tensor_tensor(
                    s_sumT[:, j, :], s_e1, s_she[:, j, :], OP.add),
                    reads=[b_e1, b_she], writes=[b_sumT])
            if ib == 0:
                dump_now("sumTd", s_sumT.rearrange("p j t -> p (j t)"), b_sumT,
                         [128, 1024], F32)
                dump_now("hrod", s_hro.rearrange("p g t -> p (g t)"), b_hro,
                         [128, 1024], BF16)
            for j8 in range(8):
                pt3, pbt3 = pk[2]
                plan.op("pe", lambda e, j=j8, p=pt3: e.transpose(
                    p[:, 0:128], s_sumT[:, j, :], c_ident[:, :]),
                    reads=[b_sumT, b_ident], writes=[pbt3])
                plan.op("dve", lambda e, j=j8, p=pt3: e.scalar_tensor_tensor(
                    s_outt[:, j * 128:(j + 1) * 128], p[:, 0:128], 1.0,
                    s_xo[:, j * 128:(j + 1) * 128], OP.bypass, OP.add),
                    reads=[pbt3, b_xo], writes=[b_outt])
                plan.op("dve", lambda e, j=j8: e.tensor_tensor(
                    s_outt[:, j * 128:(j + 1) * 128], s_outt[:, j * 128:(j + 1) * 128],
                    s_ymo[:, j * 128:(j + 1) * 128], OP.add),
                    reads=[b_ymo], writes=[b_outt], strict=True)
            if ib == 0:
                dump_now("outtd", s_outt, b_outt, [128, 1024], F32)
            plan.op("sp", lambda e, b2=ib: e.dma_start(
                out=y_out[b2 * 128:(b2 + 1) * 128, :], in_=s_outt),
                reads=[b_outt], dma=True)

    _ga = gen_ph4a()
    _na = [0]
    def _pump_a(n):
        while _na[0] < min(n, 128):
            try:
                next(_ga)
            except StopIteration:
                return
            _na[0] += 1
    for _req in gen_ph4b():
        _pump_a(_req)
    _pump_a(128)

    dbg = dict(_inline)
    if debug_outs:
        sbuf_dumps = {
            "dtin_sb": (s_dtin[:, :], b_dtin, [64, L], BF16),
            "cnt_sb": (s_cntall.rearrange("p a c -> p (a c)"), b_cntall, [128, 128 * 64], BF16),
            "idx_sb": (s_idxall.rearrange("p a c -> p (a c)"), b_idxall, [128, 128 * 8], U32),
            "lt_sb": (s_lt[0:64, :], b_ltb, [64, 512], F32),
            "toffsb": (s_toff[:, :] if False else s_toff, b_toff, [128, 64], F32),
        }
        for nm in [d for d in debug_outs if d in sbuf_dumps and d not in _inline]:
            ap2, b2, shp, dt2 = sbuf_dumps[nm]
            od = nc.dram_tensor("dbg_" + nm, shp, dt2, kind="ExternalOutput").ap()
            plan.op("sp", lambda e, s2=ap2, d2=od: e.dma_start(out=d2, in_=s2),
                    reads=[b2], dma=True)
            dbg[nm] = "dbg_" + nm
    if debug_outs:
        dumpable = dict(xn_all=xn_all, xnT_d=xnT_d, xn2T_d=xn2T_d, L_d=L_d,
                        mxT_d=mxT_d, v_d=v_d, sharedT_d=sharedT_d, uT_d=uT_d,
                        rT_d=rT_d, hxT_d=hxT_d, gtab_d=gtab_d, cidx_d=cidx_d,
                        wpr_d=wpr_d, out_all=out_all, ymoe_d=ymoe_d, toff_d=toff_d,
                        negB_d=negB_d, Cm_d=Cm_d, grow_d=grow_d, buf_d=buf_d)
        for nm in [d for d in debug_outs if d in dumpable]:
            t = dumpable[nm]
            od = nc.dram_tensor("dbg_" + nm, list(t.tensor.shape), t.dtype,
                                kind="ExternalOutput").ap()
            plan.op("sp", lambda e, s2=t, d2=od: e.dma_start(out=d2, in_=s2),
                    reads=[db(t)], dma=True)
            dbg[nm] = "dbg_" + nm
    plan.finalize(stk.enter_context(nc.Block()), sems, dsems)
    stk.close()
    return nc, dbg



_CACHE = {}


def kernel(**inputs):
    a = _host_arrays(inputs)
    a["ones_row_bf"] = np.ones((1, 128), ml_dtypes.bfloat16)
    cp = np.zeros((128, 8), np.float32)
    for i, v in enumerate([1e-5, -1.0, -0.5, 1.0 / D, 1.0, 0.0, 1e-38, 511.0]):
        cp[:, i] = v
    a["cpool"] = cp
    if "nc" not in _CACHE:
        _CACHE["nc"] = build_program()
    nc, dbg = _CACHE["nc"]
    in_maps = []
    for c in range(NCORES):
        m = dict(a)
        m["x_own"] = np.ascontiguousarray(a["x_full"][c * L:(c + 1) * L, :])
        sel = np.zeros((128, 128), np.float32)
        for j in range(16):
            sel[c * 16 + j, j] = 1.0
        m["sel16"] = sel
        in_maps.append(m)
    res = run_bass_kernel_spmd(nc, in_maps, core_ids=list(range(NCORES)))
    out = np.stack([res.results[c]["y"] for c in range(NCORES)], 0)
    return out.reshape(B, L, D).astype(np.float32)


if __name__ == "__main__":
    nc, dbg = build_program()
    tot = 0
    for e in ENGS:
        n = len([0 for _ in []])
    print("built ok")
